# version 13
# baseline (speedup 1.0000x reference)
import numpy as np
import ml_dtypes
from contextlib import ExitStack

import concourse.bass as bass
import concourse.mybir as mybir
from concourse.bass_utils import run_bass_kernel_spmd

F32 = mybir.dt.float32
BF16 = mybir.dt.bfloat16
AF = mybir.ActivationFunctionType
ALU = mybir.AluOpType
AX = mybir.AxisListType

D = 1024
DIN = 2048
NH = 32
HD = 64
DS = 64
CONV = 2176
DPROJ = 4256
MLP = 4096
NE = 8
EH = 512
EPS = 1e-5
Q = 128
C = 512
NSUB = C // Q

KT = D // 128
NT_DIN = DIN // 128
NT_CONV = 17
NT_PROJ = 34

ENGINES = ("tensor", "vector", "scalar", "gpsimd", "sync")


class Prog:
    def __init__(self):
        self.steps = []
        self.res = {}

    def step(self, eng, fn, reads=(), writes=(), waits=(), fence=False,
             dpre=False):
        sid = len(self.steps)
        deps = set()
        for n in reads:
            e = self.res.get(n)
            if e and e["w"] is not None:
                deps.add(e["w"])
        for n in writes:
            e = self.res.get(n)
            if e:
                if e["w"] is not None:
                    deps.add(e["w"])
                deps.update(e["r"])
        for n in reads:
            self.res.setdefault(n, {"w": None, "r": []})["r"].append(sid)
        for n in writes:
            self.res[n] = {"w": sid, "r": []}
        self.steps.append(dict(eng=eng, fn=fn, deps=deps, waits=list(waits),
                               fence=fence, dpre=dpre))
        return sid


def build_nc(L=8192, debug=(), serial=False):
    NCH = L // C
    nc = bass.Bass("TRN2", target_bir_lowering=False)

    x_in = nc.dram_tensor("x", [L, D], F32, kind="ExternalInput")
    w_in = nc.dram_tensor("W_in", [D, DPROJ], F32, kind="ExternalInput")
    conv_w = nc.dram_tensor("conv_w", [CONV, 4], F32, kind="ExternalInput")
    conv_b = nc.dram_tensor("conv_b", [CONV], F32, kind="ExternalInput")
    dt_bias = nc.dram_tensor("dt_bias", [NH], F32, kind="ExternalInput")
    a_log = nc.dram_tensor("A_log", [NH], F32, kind="ExternalInput")
    d_param = nc.dram_tensor("D_param", [NH], F32, kind="ExternalInput")
    norm_w = nc.dram_tensor("norm_w", [DIN], F32, kind="ExternalInput")
    w_out = nc.dram_tensor("W_out", [DIN, D], F32, kind="ExternalInput")
    rms_w = nc.dram_tensor("rms_w", [D], F32, kind="ExternalInput")
    mlp_w1 = nc.dram_tensor("mlp_w1", [D, MLP], F32, kind="ExternalInput")
    mlp_b1 = nc.dram_tensor("mlp_b1", [MLP], F32, kind="ExternalInput")
    mlp_w2 = nc.dram_tensor("mlp_w2", [MLP, D], F32, kind="ExternalInput")
    mlp_b2 = nc.dram_tensor("mlp_b2", [D], F32, kind="ExternalInput")
    gate_w = nc.dram_tensor("gate_w", [D, NE], F32, kind="ExternalInput")
    moe_w1 = nc.dram_tensor("moe_w1", [NE, D, EH], F32, kind="ExternalInput")
    moe_w2 = nc.dram_tensor("moe_w2", [NE, EH, D], F32, kind="ExternalInput")
    moe_w3 = nc.dram_tensor("moe_w3", [NE, D, EH], F32, kind="ExternalInput")
    c_i128 = nc.dram_tensor("c_i128", [128, 128], F32, kind="ExternalInput")
    c_m01 = nc.dram_tensor("c_m01", [128, 128], F32, kind="ExternalInput")
    c_r01 = nc.dram_tensor("c_r01", [NH, 4096], F32, kind="ExternalInput")
    c_slot = nc.dram_tensor("c_slot", [NH, 128], F32, kind="ExternalInput")
    c_r02 = nc.dram_tensor("c_r02", [NH, 16], F32, kind="ExternalInput")

    out_t = nc.dram_tensor("out", [L, D], F32, kind="ExternalOutput")

    dbg = {}
    for nmn, shp in [("u", [128, KT * C]), ("zx", [128, NT_DIN * C]),
                     ("xbc", [128, NT_CONV * C]), ("xs", [128, NT_DIN * C]),
                     ("bcc", [128, C]), ("dt", [NH, C]), ("ell", [NH, C]),
                     ("gw", [128, 2048]), ("xsT", [128, 2048]),
                     ("g", [128, NT_DIN * C]), ("hst", [DS, NH * HD]),
                     ("h1", [128, KT * C]), ("at", [128, KT * C]),
                     ("x1", [128, KT * C]), ("x1n", [128, KT * C]),
                     ("sc", [128, 32]), ("moe", [128, KT * C]),
                     ("dtx", [128, 2048]), ("ys0", [128, 1024]),
                     ("hst0", [DS, NH * HD]), ("g0", [128, 1024])]:
        if nmn in debug:
            dbg[nmn] = nc.dram_tensor("dbg_" + nmn, shp, F32,
                                      kind="ExternalOutput")

    x_bf = nc.dram_tensor("x_bf", [L, D], BF16, kind="Internal")
    wt_in = nc.dram_tensor("wt_in", [NT_PROJ * KT, 128, 128], BF16, kind="Internal")
    wt_out = nc.dram_tensor("wt_out", [8 * NT_DIN, 128, 128], BF16, kind="Internal")
    wt_m1 = nc.dram_tensor("wt_m1", [32 * KT, 128, 128], BF16, kind="Internal")
    wt_m2 = nc.dram_tensor("wt_m2", [8 * 32, 128, 128], BF16, kind="Internal")
    wt_e1 = nc.dram_tensor("wt_e1", [NE, 4 * KT, 128, 128], BF16, kind="Internal")
    wt_e2 = nc.dram_tensor("wt_e2", [NE, 8 * 4, 128, 128], BF16, kind="Internal")
    wt_e3 = nc.dram_tensor("wt_e3", [NE, 4 * KT, 128, 128], BF16, kind="Internal")
    ell_dram = nc.dram_tensor("ell_dram", [NH, C], F32, kind="Internal")
    sc_dram = nc.dram_tensor("sc_dram", [32, 128], F32, kind="Internal")

    P = Prog()
    with ExitStack() as st:
        def sb(name, shape, dt=BF16):
            return st.enter_context(nc.sbuf_tensor(name, shape, dt))

        i128f = sb("i128f", [128, 128], F32)
        i128b = sb("i128b", [128, 128], BF16)
        m01b = sb("m01b", [128, 128], BF16)
        ones_f = sb("ones_f", [1, 128], F32)
        ones_b = sb("ones_b", [1, 128], BF16)
        ones_bc = sb("ones_bc", [128, 1], BF16)
        cdiag = sb("cdiag", [128, NT_CONV * 4 * 128], BF16)
        convb_c = sb("convb_c", [128, NT_CONV], F32)
        cw_sb = sb("cw_sb", [128, NT_CONV * 4], F32)
        b1cols = sb("b1cols", [128, 32], F32)
        b2cols = sb("b2cols", [128, KT], F32)
        gatew_sb = sb("gatew_sb", [128, KT * NE], BF16)
        acol = sb("acol", [NH, 1], F32)
        dtb_col = sb("dtb_col", [NH, 1], F32)
        dp_col = sb("dp_col", [NH, 1], F32)
        dbc16 = sb("dbc16", [128, 16], F32)
        small = sb("small", [NH, 640], F32)
        nwcols = sb("nwcols", [128, 24], F32)
        cst = sb("cst", [128, 4], F32)

        x_cm = sb("x_cm", [128, KT * C])
        u_cm = sb("u_cm", [128, KT * C])
        z_cm = sb("z_cm", [128, NT_DIN * C])
        xbc_cm = sb("xbc_cm", [128, NT_CONV * (C + 4)])
        xs_cm = sb("xs_cm", [128, NT_DIN * C])
        bcc_cm = sb("bcc_cm", [128, C])
        ccc_cm = sb("ccc_cm", [DS, C])
        g_cm = sb("g_cm", [128, NT_DIN * C])
        h1_cm = sb("h1_cm", [128, KT * C])
        hn_cm = sb("hn_cm", [128, KT * C])
        x1_cm = sb("x1_cm", [128, KT * C])
        moe_cm = sb("moe_cm", [128, KT * C])

        dtf = sb("dtf", [NH, C], F32)
        dta = sb("dta", [NH, C], F32)
        ellf = sb("ellf", [NH, C], F32)
        zero32 = sb("zero32", [NH, Q], F32)
        lam_cm = sb("lam_cm", [NH, C])
        esc_cm = sb("esc_cm", [NH, C])
        dt16_cm = sb("dt16_cm", [NH, C])
        tailbuf = sb("tailbuf", [128, NT_CONV * 3])

        lamT = sb("lamT", [128, 32])
        lT32all = sb("lT32all", [128, 128], F32)
        escT = sb("escT", [128, 32])
        dtT = sb("dtT", [128, 32])
        dcol16 = sb("dcol16", [NH, NSUB])
        drow4 = sb("drow4", [1, 128])
        dexp = sb("dexp", [1, 2048])
        gtm4 = sb("gtm4", [128, NSUB * Q])
        bcT = sb("bcT", [128, 64])
        h16 = [sb("h16a", [DS, NH * HD]), sb("h16b", [DS, NH * HD])]
        e16h = [xbc_cm[:, 0:2048], h1_cm[:, 2048:4096]]
        gw16a = xbc_cm[:, 2048:4096]
        gw16b = xbc_cm[:, 4096:6144]
        dsc32 = xbc_cm[:, 6144:8192].bitcast(F32)
        xsT = x1_cm[:, 0:2048]
        dtx = x1_cm[:, 2048:4096]
        dtxs = moe_cm[:, 0:2048]
        ya16 = moe_cm[:, 2048:3072]
        ystm = moe_cm[:, 3072:4096]
        slsb = h1_cm[0:64, 0:1024]

        r_m = sb("r_m", [1, C], F32)
        r_v = sb("r_v", [1, C], F32)
        r_t = sb("r_t", [1, C], F32)
        r_rstd = sb("r_rstd", [1, C])
        r_mr = sb("r_mr", [1, C])
        bc_r = sb("bc_r", [128, C])
        bc_mr = sb("bc_mr", [128, C])

        logT = sb("logT", [128, 32], F32)
        rt_m = sb("rt_m", [128, 8], F32)
        rt_e = sb("rt_e", [128, 32], F32)
        rt_s = sb("rt_s", [128, 32], F32)
        sc_cmT = sb("sc_cmT", [32, 128], F32)
        srow = sb("srow", [1, C])
        sbc = h1_cm[:, 0:512]
        p1s = u_cm[:, 0:2048]
        p3s = u_cm[:, 2048:4096]
        x2t32 = [hn_cm[:, 0:2048].bitcast(F32),
                 hn_cm[:, 2048:4096].bitcast(F32)]

        NRING = 4
        wring = [sb(f"wring{i}", [128, 4096]) for i in range(NRING)]

        pA = st.enter_context(nc.psum_tensor("pA", [128, 512], F32))
        pB = st.enter_context(nc.psum_tensor("pB", [128, 512], F32))
        pC = st.enter_context(nc.psum_tensor("pC", [128, 1024], F32))
        pD = st.enter_context(nc.psum_tensor("pD", [128, 1024], F32))
        pT = st.enter_context(nc.psum_tensor("pT", [128, 1024], BF16))
        pU = st.enter_context(nc.psum_tensor("pU", [128, 1024], BF16))

        esem = {e: st.enter_context(nc.semaphore("sem_" + e)) for e in ENGINES}
        q_w = st.enter_context(nc.semaphore("q_w"))
        q_s = st.enter_context(nc.semaphore("q_s"))
        q_g = st.enter_context(nc.semaphore("q_g"))
        QS = {"q_w": q_w, "q_s": q_s, "q_g": q_g}
        qcnt = {"q_w": 0, "q_g": 0, "q_s": 0}
        qrt = {"q_w": 0, "q_g": 0, "q_s": 0}

        def qdma(eng, qname, out, in_, **kw):
            eng.dma_start(out, in_, **kw).then_inc(QS[qname], 16)
            qrt[qname] += 16
            return qrt[qname]

        def qdecl(qname, n=1):
            pre = qcnt[qname]
            P.steps[-1]["waits"].append((qname, pre))
            qcnt[qname] += 16 * n
            return qcnt[qname]

        def wide(t, i, cols=None):
            a, b = cols if cols else (0, C)
            return t[:, i * C + a:i * C + b]

        def w3(t, tiles, cols):
            a, b = tiles
            c0, c1 = cols
            return t[:, :].rearrange("p (i c) -> p i c", c=C)[:, a:b, c0:c1]

        def pro_g(eng):
            qdma(eng, "q_g", i128b[:, :], c_i128.ap())
            qdma(eng, "q_g", m01b[:, :], c_m01.ap())
            qdma(eng, "q_g", cw_sb[:, :].rearrange("p (i k) -> p i k", k=4),
                 conv_w.ap().rearrange("(i p) k -> p i k", p=128))
            qdma(eng, "q_g", convb_c[:, :],
                 conv_b.ap().rearrange("(i p) -> p i", p=128),
                 allow_slow_non_contiguous=True)
            qdma(eng, "q_g", b1cols[:, :],
                 mlp_b1.ap().rearrange("(i p) -> p i", p=128),
                 allow_slow_non_contiguous=True)
            qdma(eng, "q_g", b2cols[:, :],
                 mlp_b2.ap().rearrange("(i p) -> p i", p=128),
                 allow_slow_non_contiguous=True)
            qdma(eng, "q_g", gatew_sb[:, :].rearrange("p (k e) -> p k e", e=NE),
                 gate_w.ap().rearrange("(k p) e -> p k e", p=128))
            qdma(eng, "q_g", x_bf.ap(), x_in.ap())
        P.step("gpsimd", pro_g, writes=["i128b", "m01b", "cw", "convb", "b1c",
                                        "b2c", "gw_sb", "x_bf"])
        PRO_G1 = qdecl("q_g", 8)

        def pro_s(eng):
            qdma(eng, "q_s", i128f[:, :], c_i128.ap())
            qdma(eng, "q_s", acol[:, :], a_log.ap()[:, None])
            qdma(eng, "q_s", dtb_col[:, :], dt_bias.ap()[:, None])
            qdma(eng, "q_s", dp_col[:, :], d_param.ap()[:, None])
            qdma(eng, "q_s", small[:, 0:128], c_slot.ap())
            qdma(eng, "q_s", small[:, 128:144], c_r02.ap())
            qdma(eng, "q_s", nwcols[:, 0:16],
                 norm_w.ap().rearrange("(i p) -> p i", p=128),
                 allow_slow_non_contiguous=True)
            qdma(eng, "q_s", nwcols[:, 16:24],
                 rms_w.ap().rearrange("(i p) -> p i", p=128),
                 allow_slow_non_contiguous=True)
        P.step("scalar", pro_s, writes=["i128f", "acol", "dtb", "dp",
                                        "small", "nwcols"])
        PRO_S1 = qdecl("q_s", 8)

        def tile_casts(eng):
            for oc in range(NT_PROJ):
                ocw = 128 if oc < NT_PROJ - 1 else 32
                qdma(eng, "q_g", wt_in.ap()[oc * KT:(oc + 1) * KT, :, 0:ocw],
                     w_in.ap()[:, oc * 128:oc * 128 + ocw]
                     .rearrange("(k p) c -> k p c", p=128))
            for oc in range(8):
                for kg in range(2):
                    qdma(eng, "q_g",
                         wt_out.ap()[oc * NT_DIN + kg * 8:oc * NT_DIN + kg * 8 + 8],
                         w_out.ap()[kg * 1024:(kg + 1) * 1024,
                                    oc * 128:(oc + 1) * 128]
                         .rearrange("(k p) c -> k p c", p=128))
            for oc in range(32):
                qdma(eng, "q_g", wt_m1.ap()[oc * KT:(oc + 1) * KT],
                     mlp_w1.ap()[:, oc * 128:(oc + 1) * 128]
                     .rearrange("(k p) c -> k p c", p=128))
            for oc in range(8):
                for kg in range(4):
                    qdma(eng, "q_g",
                         wt_m2.ap()[oc * 32 + kg * 8:oc * 32 + kg * 8 + 8],
                         mlp_w2.ap()[kg * 1024:(kg + 1) * 1024,
                                     oc * 128:(oc + 1) * 128]
                         .rearrange("(k p) c -> k p c", p=128))
            for e in range(NE):
                for oc in range(4):
                    qdma(eng, "q_g", wt_e1.ap()[e, oc * KT:(oc + 1) * KT],
                         moe_w1.ap()[e][:, oc * 128:(oc + 1) * 128]
                         .rearrange("(k p) c -> k p c", p=128))
                    qdma(eng, "q_g", wt_e3.ap()[e, oc * KT:(oc + 1) * KT],
                         moe_w3.ap()[e][:, oc * 128:(oc + 1) * 128]
                         .rearrange("(k p) c -> k p c", p=128))
                for oc in range(8):
                    qdma(eng, "q_g", wt_e2.ap()[e, oc * 4:(oc + 1) * 4],
                         moe_w2.ap()[e][:, oc * 128:(oc + 1) * 128]
                         .rearrange("(k p) c -> k p c", p=128))
        P.step("gpsimd", tile_casts, writes=["wt_all"])
        WT_DONE = qdecl("q_g", 242)

        def v_init(eng):
            eng.memset(ones_f[:, :], 1.0)
            eng.memset(ones_b[:, :], 1.0)
            eng.memset(ones_bc[:, :], 1.0)
            eng.memset(h16[0][:, :], 0.0)
            eng.memset(h16[1][:, :], 0.0)
            eng.memset(tailbuf[:, :], 0.0)
            eng.memset(cst[:, 0:1], EPS)
            eng.memset(cst[:, 1:2], 1.0)
            eng.memset(cst[:, 2:3], -1.0)
            eng.memset(cst[:, 3:4], -0.5)
            eng.memset(zero32[:, :], 0.0)
            return eng.memset(wring[0][:, :], 0.0)
        P.step("vector", v_init, writes=["ones", "h16a", "h16b", "tailbuf",
                                         "z32", "wr0"])

        def g_pad(eng):
            qdma(eng, "q_g",
                 wt_in.ap()[(NT_PROJ - 1) * KT:NT_PROJ * KT, :, 32:128]
                 .rearrange("k p c -> p k c"),
                 wring[0][:, 0:KT * 96].rearrange("p (k c) -> p k c", c=96))
        P.step("gpsimd", g_pad, reads=["wr0", "wt_all"], writes=["wt_pad"],
               waits=[("q_g", WT_DONE)])
        PAD_DONE = qdecl("q_g", 1)

        def s_acol(eng):
            return eng.activation(acol[:, :], acol[:, :], AF.Exp)
        P.step("scalar", s_acol, reads=["acol"], writes=["acol"],
               waits=[("q_s", PRO_S1)])

        def v_der(eng):
            eng.tensor_scalar_mul(acol[:, :], acol[:, :], -1.0)
            for i in range(NT_CONV):
                for k in range(4):
                    eng.tensor_scalar_mul(
                        cdiag[:, (i * 4 + k) * 128:(i * 4 + k + 1) * 128],
                        i128b[:, :], cw_sb[:, i * 4 + k:i * 4 + k + 1])
            return eng.tensor_scalar_mul(small[:, 128:144], small[:, 128:144],
                                         dp_col[:, :])
        P.step("vector", v_der, reads=["acol", "i128b", "cw", "dp", "small"],
               writes=["acol", "cdiag", "small"],
               waits=[("q_g", PRO_G1), ("q_s", PRO_S1)])

        def t_dbc(eng):
            return eng.matmul(pA[:, 0:16], small[:, 0:128], small[:, 128:144],
                              start=True, stop=True)
        P.step("tensor", t_dbc, reads=["small"], writes=["pA"])

        def v_dbc(eng):
            return eng.tensor_copy(dbc16[:, :], pA[:, 0:16])
        P.step("vector", v_dbc, reads=["pA"], writes=["dbc16"])

        for r in range(NT_DIN):
            def g_l(eng, r=r):
                qdma(eng, "q_g",
                     wring[r % 2][:, 0:1024].rearrange("p (o c) -> p o c", c=128),
                     wt_out.ap().rearrange("(o k) p c -> k p o c", k=NT_DIN)[r])
            P.step("gpsimd", g_l, reads=["wt_all"], writes=[f"wr{r % 2}"],
                   waits=[("q_g", WT_DONE)])
            wv = qdecl("q_g", 1)
            def v_s(eng, r=r):
                return eng.tensor_scalar_mul(wring[r % 2][:, 0:1024],
                                             wring[r % 2][:, 0:1024],
                                             nwcols[:, r:r + 1])
            P.step("vector", v_s, reads=[f"wr{r % 2}", "nwcols"],
                   writes=[f"wr{r % 2}"], waits=[("q_g", wv)])
            def g_s(eng, r=r):
                qdma(eng, "q_g",
                     wt_out.ap().rearrange("(o k) p c -> k p o c", k=NT_DIN)[r],
                     wring[r % 2][:, 0:1024].rearrange("p (o c) -> p o c", c=128))
            P.step("gpsimd", g_s, reads=[f"wr{r % 2}"], writes=["wt_all"])
            qdecl("q_g", 1)
        for r in range(KT):
            def g_l(eng, r=r):
                qdma(eng, "q_g",
                     wring[r % 2][:, 0:4096].rearrange("p (o c) -> p o c", c=128),
                     wt_m1.ap().rearrange("(o k) p c -> k p o c", k=KT)[r])
            P.step("gpsimd", g_l, reads=["wt_all"], writes=[f"wr{r % 2}"])
            wv = qdecl("q_g", 1)
            def v_s(eng, r=r):
                return eng.tensor_scalar_mul(wring[r % 2][:, 0:4096],
                                             wring[r % 2][:, 0:4096],
                                             nwcols[:, 16 + r:17 + r])
            P.step("vector", v_s, reads=[f"wr{r % 2}", "nwcols"],
                   writes=[f"wr{r % 2}"], waits=[("q_g", wv)])
            def g_s(eng, r=r):
                qdma(eng, "q_g",
                     wt_m1.ap().rearrange("(o k) p c -> k p o c", k=KT)[r],
                     wring[r % 2][:, 0:4096].rearrange("p (o c) -> p o c", c=128))
            P.step("gpsimd", g_s, reads=[f"wr{r % 2}"], writes=["wt_all"])
            qdecl("q_g", 1)
        PRO_GD = qcnt["q_g"]

        ring_i = [0]

        def ring_load(srcs):
            slot = ring_i[0] % NRING
            qsel = "q_w" if ring_i[0] % 2 == 0 else "q_g"
            ring_i[0] += 1
            def fn(eng, srcs=srcs, slot=slot, qsel=qsel):
                for ncols, src in srcs:
                    qdma(eng, qsel,
                         wring[slot][:, 0:ncols].rearrange("p (t c) -> p t c",
                                                           c=128),
                         src.rearrange("t p c -> p t c"))
            P.step("sync" if qsel == "q_w" else "gpsimd", fn,
                   reads=["wt_all"], writes=[f"wr{slot}"],
                   waits=[("q_g", PRO_GD)])
            return slot, (qsel, qdecl(qsel, len(srcs)))

        def chunk_pieces():
            ps = []
            ps.append([(2 * KT * 128, wt_in.ap()[32 * KT:34 * KT])])
            for p in range(8):
                ps.append([(4 * KT * 128,
                            wt_in.ap()[p * 4 * KT:(p * 4 + 4) * KT])])
            for p in range(4):
                ps.append([(2 * NT_DIN * 128,
                            wt_out.ap()[p * 2 * NT_DIN:(p + 1) * 2 * NT_DIN])])
            for p in range(8):
                ps.append([(4 * KT * 128, wt_m1.ap()[p * 4 * KT:(p + 1) * 4 * KT])])
            for p in range(8):
                ps.append([(32 * 128, wt_m2.ap()[p * 32:(p + 1) * 32])])
            for e in range(NE):
                ps.append([(32 * 128, wt_e1.ap()[e])])
                ps.append([(32 * 128, wt_e3.ap()[e])])
            for og in range(2):
                for e in range(NE):
                    ps.append([(32 * 128, wt_e2.ap()[e])])
            return ps

        def stats(src, srcn, ntiles, with_mean, nfeat, sq_dst, sqn,
                  defer_bc=False, sq_eng="vector", pre_done=False):
            def v_sq(eng):
                last = None
                for i in range(ntiles):
                    last = eng.tensor_mul(wide(sq_dst, i), wide(src, i),
                                          wide(src, i))
                return last
            if not pre_done:
                P.step(sq_eng, v_sq, reads=[srcn], writes=[sqn])

            def t_mm(eng):
                last = None
                if with_mean:
                    for i in range(ntiles):
                        last = eng.matmul(pA[0:1, :], ones_bc[:, :],
                                          wide(src, i), start=(i == 0),
                                          stop=(i == ntiles - 1))
                for i in range(ntiles):
                    last = eng.matmul(pB[0:1, :], ones_bc[:, :],
                                      wide(sq_dst, i), start=(i == 0),
                                      stop=(i == ntiles - 1))
                return last
            if not pre_done:
                P.step("tensor", t_mm, reads=[srcn, sqn, "ones"],
                       writes=["pA", "pB"])

            def v_row(eng):
                if with_mean:
                    eng.tensor_scalar_mul(r_m[:, :], pA[0:1, :], 1.0 / nfeat)
                    eng.tensor_mul(r_t[:, :], r_m[:, :], r_m[:, :])
                    return eng.scalar_tensor_tensor(r_v[:, :], pB[0:1, :],
                                                    1.0 / nfeat, r_t[:, :],
                                                    ALU.mult, ALU.subtract)
                return eng.tensor_scalar_mul(r_v[:, :], pB[0:1, :], 1.0 / nfeat)
            P.step("vector", v_row, reads=["pA", "pB"], writes=["rows"])

            def s_row(eng):
                eng.activation(r_t[:, :], r_v[:, :], AF.Ln, bias=cst[0:1, 0:1])
                return eng.activation(r_rstd[:, :], r_t[:, :], AF.Exp,
                                      scale=cst[0:1, 3:4])
            P.step("scalar", s_row, reads=["rows", "cst"], writes=["rows"])

            if with_mean:
                def v_mr(eng):
                    return eng.tensor_mul(r_mr[:, :], r_m[:, :], r_rstd[:, :])
                P.step("vector", v_mr, reads=["rows"], writes=["rows"])

            def bc_part():
                def t_bc(eng):
                    last = eng.matmul(pA[:, :], ones_b[:, :], r_rstd[:, :],
                                      start=True, stop=True)
                    if with_mean:
                        last = eng.matmul(pB[:, :], ones_b[:, :], r_mr[:, :],
                                          start=True, stop=True)
                    return last
                P.step("tensor", t_bc, reads=["rows", "ones"],
                       writes=["pA", "pB"])

                def v_bc(eng):
                    last = eng.tensor_copy(bc_r[:, :], pA[:, :])
                    if with_mean:
                        last = eng.tensor_copy(bc_mr[:, :], pB[:, :])
                    return last
                P.step("vector", v_bc, reads=["pA", "pB"], writes=["bc"])
            if defer_bc:
                return bc_part
            bc_part()

        def dump(name, ap_fn, reads):
            if name not in dbg:
                return
            def fn(eng, name=name):
                v = qdma(eng, "q_g", dbg[name].ap(), ap_fn())
                eng.wait_ge(QS["q_g"], v)
            P.step("gpsimd", fn, reads=list(reads), writes=["dbgq"])
            qdecl("q_g", 1)

        def emit_ssd_g(ci):
            def t_g4(eng):
                last = None
                for q in range(NSUB):
                    last = eng.matmul(pA[:, q * Q:(q + 1) * Q],
                                      bcc_cm[0:64, q * Q:(q + 1) * Q],
                                      ccc_cm[:, q * Q:(q + 1) * Q],
                                      start=True, stop=True)
                return last
            P.step("tensor", t_g4, reads=["bcc", "ccc"], writes=["pA"])

            def v_g4(eng):
                return eng.tensor_tensor(
                    gtm4[:, :].rearrange("p (q t) -> p q t", t=Q),
                    pA[:, :].rearrange("p (q t) -> p q t", t=Q),
                    m01b[:, None, :].broadcast_to([128, NSUB, Q]), ALU.mult)
            P.step("vector", v_g4, reads=["pA", "m01b"], writes=["gtm"])

        def emit_ssd_chunk(ci):
            def t_ltall(eng):
                last = None
                for q in range(NSUB):
                    last = eng.transpose(pB[:, 32 * q:32 * (q + 1)],
                                         ellf[:, q * Q:(q + 1) * Q],
                                         i128f[0:32, 0:32])
                return last
            P.step("tensor", t_ltall, reads=["ellf", "i128f"], writes=["pB"])

            def v_ltall(eng):
                return eng.tensor_copy(lT32all[:, :], pB[:, 0:128])
            P.step("vector", v_ltall, reads=["pB"], writes=["lT32"])

            def v_dc(eng):
                return eng.tensor_copy(
                    dcol16[:, :, None],
                    lam_cm[:, :].rearrange("p (q t) -> p q t", t=Q)
                    [:, :, Q - 1:Q])
            P.step("vector", v_dc, reads=["lam"], writes=["dcol"])

            def t_dr4(eng):
                last = None
                for q in range(NSUB):
                    last = eng.transpose(pU[0:1, 32 * q:32 * (q + 1)],
                                         dcol16[:, q:q + 1],
                                         i128b[0:32, 0:32])
                return last
            P.step("tensor", t_dr4, reads=["dcol", "i128b"], writes=["pU"])

            def v_dr4(eng):
                return eng.tensor_copy(drow4[:, :], pU[0:1, 0:128])
            P.step("vector", v_dr4, reads=["pU"], writes=["drow"])

        def emit_ssd_xs(ci, qn):
            qs = (qn * Q, qn * Q + Q)

            def t_w1(eng):
                last = None
                for i in range(NT_DIN):
                    pp = pT if i < 8 else pU
                    last = eng.transpose(pp[:, (i % 8) * 128:(i % 8 + 1) * 128],
                                         wide(xs_cm, i, qs), i128b[:, :])
                return last
            P.step("tensor", t_w1, reads=["xs", "i128b"], writes=["pT", "pU"])

            def s_w1(eng):
                eng.activation(xsT[:, 0:1024], pT[:, :], AF.Copy)
                return eng.activation(xsT[:, 1024:2048], pU[:, :], AF.Copy)
            P.step("scalar", s_w1, reads=["pT", "pU"], writes=["xsT"])

        def emit_ssd_early(ci, qn, elwv, xs_done=False):
            qs = (qn * Q, qn * Q + Q)
            if not xs_done:
                emit_ssd_xs(ci, qn)

            def t_w2(eng):
                eng.transpose(pT[:, 0:64], bcc_cm[0:64, qs[0]:qs[1]],
                              i128b[0:64, 0:64])
                eng.transpose(pT[:, 64:96], lam_cm[:, qs[0]:qs[1]],
                              i128b[0:32, 0:32])
                eng.transpose(pT[:, 96:128], esc_cm[:, qs[0]:qs[1]],
                              i128b[0:32, 0:32])
                return eng.transpose(pT[:, 128:160], dt16_cm[:, qs[0]:qs[1]],
                                     i128b[0:32, 0:32])
            P.step("tensor", t_w2, reads=["bcc", "lam", "esc", "dt16", "i128b"],
                   writes=["pT"])

            for hb in range(2):
                for qq in range(2):
                    def s_lb(eng, hb=hb, qq=qq):
                        eng.wait_ge(QS["q_s"], elwv)
                        qdma(eng, "q_s",
                             dsc32[:, :].rearrange("p (h t) -> p h t", t=Q),
                             ell_dram.ap()[hb * 16 + qq * 8:hb * 16 + (qq + 1) * 8,
                                           qs[0]:qs[1]][None, :, :]
                             .broadcast_to([128, 8, Q]))
                    anchor = ["xbc"] if (qn == 0 and hb == 0 and qq == 0) else []
                    P.step("scalar", s_lb, reads=["ell_dram"],
                           writes=["dsc"] + anchor)
                    lbwv = qdecl("q_s", 1)
                    def v_d(eng, qn=qn, hb=hb, qq=qq, lbwv=lbwv):
                        eng.wait_ge(QS["q_s"], lbwv)
                        return eng.tensor_tensor(
                            dsc32[:, :].rearrange("p (h t) -> p h t", t=Q),
                            dsc32[:, :].rearrange("p (h t) -> p h t", t=Q),
                            lT32all[:, 32 * qn + 16 * hb + 8 * qq:
                                    32 * qn + 16 * hb + 8 * (qq + 1), None]
                            .broadcast_to([128, 8, Q]), ALU.subtract)
                    P.step("vector", v_d, reads=["dsc", "lT32"],
                           writes=["dsc"],
                           dpre=(qn == 0 and hb == 0 and qq == 0))
                    def s_exp(eng, hb=hb, qq=qq):
                        return eng.activation(
                            e16h[hb][:, qq * 1024:(qq + 1) * 1024],
                            dsc32[:, :], AF.Exp)
                    P.step("scalar", s_exp, reads=["dsc"],
                           writes=(["e16a"] if hb == 0 else ["e16b", "h1"]))

        def emit_ssd_late(ci, qn):
            qs = (qn * Q, qn * Q + Q)
            def s_w2(eng):
                eng.activation(bcT[:, :], pT[:, 0:64], AF.Copy)
                eng.activation(lamT[:, :], pT[:, 64:96], AF.Copy)
                eng.activation(escT[:, :], pT[:, 96:128], AF.Copy)
                return eng.activation(dtT[:, :], pT[:, 128:160], AF.Copy)
            P.step("scalar", s_w2, reads=["pT"],
                   writes=["bcT", "lamT", "escT", "dtT"])
            def v_w2(eng):
                eng.tensor_tensor(dtx[:, :].rearrange("p (h v) -> p h v", v=HD),
                                  xsT[:, :].rearrange("p (h v) -> p h v", v=HD),
                                  dtT[:, :, None].broadcast_to([128, NH, HD]),
                                  ALU.mult)
                return eng.tensor_tensor(
                    dtxs[:, :].rearrange("p (h v) -> p h v", v=HD),
                    dtx[:, :].rearrange("p (h v) -> p h v", v=HD),
                    escT[:, :, None].broadcast_to([128, NH, HD]), ALU.mult)
            P.step("vector", v_w2, reads=["dtT", "escT", "xsT"],
                   writes=["dtx", "moe"])
            if ci == 0 and qn == 0:
                dump("dtx", lambda: dtx[:, :], ["dtx"])
            for hb, gwd in ((0, gw16a), (1, gw16b)):
                def v_gw(eng, qn=qn, hb=hb, gwd=gwd):
                    return eng.scalar_tensor_tensor(
                        gwd[:, :].rearrange("p (h t) -> p h t", t=Q),
                        e16h[hb][:, :].rearrange("p (h t) -> p h t", t=Q),
                        1.0,
                        gtm4[:, qn * Q:(qn + 1) * Q][:, None, :]
                        .broadcast_to([128, 16, Q]), ALU.min, ALU.mult)
                P.step("vector", v_gw,
                       reads=["e16a" if hb == 0 else "e16b", "gtm"],
                       writes=["gw16a" if hb == 0 else "gw16b"],
                       dpre=(qn == 0 and hb == 0))
            def v_dx(eng, qn=qn):
                return eng.tensor_copy(
                    dexp[0:1, :].rearrange("p (h v) -> p h v", v=HD),
                    drow4[0:1, 32 * qn:32 * qn + 32, None]
                    .broadcast_to([1, 32, HD]))
            P.step("vector", v_dx, reads=["drow"], writes=["dexp"],
                   dpre=(qn == 0))
            if ci == 0 and qn == 0:
                dump("gw", lambda: gw16a[:, :], ["gw16a"])
                dump("xsT", lambda: xsT[:, :], ["xsT"])

        def emit_ssd_y(ci, qn):
            qs = (qn * Q, qn * Q + Q)
            hold, hnew = h16[qn % 2], h16[(qn + 1) % 2]
            holdn = "h16a" if qn % 2 == 0 else "h16b"
            hnewn = "h16b" if qn % 2 == 0 else "h16a"
            for hb, gwd in ((0, gw16a), (1, gw16b)):
                def t_yi(eng, hb=hb, gwd=gwd):
                    last = None
                    for hl in range(16):
                        h = 16 * hb + hl
                        pp = pC if hl % 2 == 0 else pD
                        last = eng.matmul(
                            pp[0:64, (hl // 2) * 128:(hl // 2 + 1) * 128],
                            dtx[:, HD * h:HD * h + HD],
                            gwd[:, Q * hl:Q * (hl + 1)],
                            start=(hl in (0, 1, 8, 9)), stop=False)
                    return last
                P.step("tensor", t_yi,
                       reads=["dtx", "gw16a" if hb == 0 else "gw16b"],
                       writes=["pC", "pD"])

                def t_ys(eng, hb=hb, hold=hold):
                    eng.matmul(pA[:, :], ccc_cm[:, qs[0]:qs[1]],
                               hold[:, hb * 1024:hb * 1024 + 512],
                               start=True, stop=True)
                    return eng.matmul(pB[:, :], ccc_cm[:, qs[0]:qs[1]],
                                      hold[:, hb * 1024 + 512:
                                           hb * 1024 + 1024],
                                      start=True, stop=True)
                P.step("tensor", t_ys, reads=["ccc", holdn],
                       writes=["pA", "pB"])

                def v_ys(eng, hb=hb):
                    eng.tensor_tensor(
                        ystm[:, 0:512].rearrange("p (h v) -> p h v", v=HD),
                        pA[:, :].rearrange("p (h v) -> p h v", v=HD),
                        lamT[:, 16 * hb:16 * hb + 8, None]
                        .broadcast_to([128, 8, HD]), ALU.mult)
                    return eng.tensor_tensor(
                        ystm[:, 512:1024].rearrange("p (h v) -> p h v", v=HD),
                        pB[:, :].rearrange("p (h v) -> p h v", v=HD),
                        lamT[:, 16 * hb + 8:16 * hb + 16, None]
                        .broadcast_to([128, 8, HD]), ALU.mult)
                P.step("vector", v_ys, reads=["pA", "pB", "lamT"],
                       writes=["moe"])
                if ci == 0 and qn == 0 and hb == 0:
                    dump("ys0", lambda: ystm[:, :], ["moe"])

                def t_yt(eng, hb=hb):
                    last = None
                    for hl in range(16):
                        pp = pC if hl % 2 == 0 else pD
                        last = eng.matmul(
                            pp[0:64, (hl // 2) * 128:(hl // 2 + 1) * 128],
                            ystm[:, HD * hl:HD * hl + HD], i128b[:, :],
                            start=False, stop=(hl in (6, 7, 14, 15)))
                    return last
                P.step("tensor", t_yt, reads=["moe", "i128b"],
                       writes=["pC", "pD"])

                def t_sl(eng, hb=hb):
                    eng.matmul(pA[0:64, :], bcT[:, :],
                               dtxs[:, hb * 1024:hb * 1024 + 512],
                               start=True, stop=True)
                    return eng.matmul(pB[0:64, :], bcT[:, :],
                                      dtxs[:, hb * 1024 + 512:
                                           hb * 1024 + 1024],
                                      start=True, stop=True)
                P.step("tensor", t_sl, reads=["bcT", "moe"],
                       writes=["pA", "pB"])

                def v_as1(eng, qn=qn, hb=hb):
                    eng.tensor_tensor(
                        ya16[:, :].rearrange("p (i t) -> p i t", t=Q),
                        w3(xs_cm, (8 * hb, 8 * hb + 8), qs),
                        dbc16[:, 8 * hb:8 * hb + 8, None]
                        .broadcast_to([128, 8, Q]), ALU.mult)
                    eng.tensor_add(ya16[0:64, :], ya16[0:64, :], pC[0:64, :])
                    return eng.tensor_add(ya16[64:128, :], ya16[64:128, :],
                                          pD[0:64, :])
                P.step("vector", v_as1, reads=["pC", "pD", "xs", "dbc16"],
                       writes=["moe"])
                if ci == 0 and qn == 0 and hb == 0:
                    dump("g0", lambda: ya16[:, :], ["moe"])

                def t_dm(eng, hb=hb):
                    pp = pC if hb == 0 else pD
                    eng.matmul(pp[0:64, 0:512], ones_b[0:1, 0:64],
                               dexp[0:1, hb * 1024:hb * 1024 + 512],
                               start=True, stop=True)
                    return eng.matmul(pp[0:64, 512:1024], ones_b[0:1, 0:64],
                                      dexp[0:1, hb * 1024 + 512:
                                           hb * 1024 + 1024],
                                      start=True, stop=True)
                P.step("tensor", t_dm, reads=["dexp", "ones"],
                       writes=["pC" if hb == 0 else "pD"])

                def v_hu(eng, hb=hb, hold=hold, hnew=hnew):
                    pp = pC if hb == 0 else pD
                    eng.tensor_tensor(ystm[0:64, :],
                                      hold[:, hb * 1024:(hb + 1) * 1024],
                                      pp[0:64, :], ALU.mult)
                    eng.tensor_add(hnew[:, hb * 1024:hb * 1024 + 512],
                                   ystm[0:64, 0:512], pA[0:64, :])
                    return eng.tensor_add(hnew[:, hb * 1024 + 512:
                                               hb * 1024 + 1024],
                                          ystm[0:64, 512:1024], pB[0:64, :])
                P.step("vector", v_hu,
                       reads=["pA", "pB", holdn, "pC" if hb == 0 else "pD"],
                       writes=[hnewn, "moe"])

                def v_as2(eng, qn=qn, hb=hb):
                    return eng.tensor_tensor(
                        w3(g_cm, (8 * hb, 8 * hb + 8), qs),
                        ya16[:, :].rearrange("p (i t) -> p i t", t=Q),
                        w3(z_cm, (8 * hb, 8 * hb + 8), qs), ALU.mult)
                P.step("vector", v_as2, reads=["moe", "z"], writes=["g"])

        store_wv = {0: 0, 1: 0}
        for ci in range(NCH):
            tok0 = ci * C
            pieces = chunk_pieces()
            pinfo = {}
            nxt = [0]

            def issue():
                if nxt[0] < len(pieces):
                    pinfo[nxt[0]] = ring_load(pieces[nxt[0]])
                    nxt[0] += 1
            for _ in range(NRING):
                issue()

            def cw(idx):
                slot, (qn_, wv) = pinfo[idx]
                return slot, [(qn_, wv)]

            def s_x(eng, tok0=tok0):
                for k in range(KT):
                    qdma(eng, "q_s", wide(x_cm, k),
                         x_bf.ap()[tok0:tok0 + C, k * 128:(k + 1) * 128],
                         transpose=True)
            P.step("scalar", s_x, reads=["x_bf"], writes=["x"],
                   waits=[("q_g", PRO_G1)])
            xwv = qdecl("q_s", KT)

            def v_tail(eng):
                last = None
                for i in range(NT_CONV):
                    last = eng.tensor_copy(
                        xbc_cm[:, i * (C + 4):i * (C + 4) + 3],
                        tailbuf[:, i * 3:(i + 1) * 3])
                return last
            P.step("vector", v_tail, reads=["tailbuf", "xbc"],
                   writes=["xbc_tail"])

            def v_xw(eng, xwv=xwv):
                eng.wait_ge(QS["q_s"], xwv)
                return None
            P.step("vector", v_xw, reads=["x"], writes=["x"])

            stats(x_cm, "x", KT, True, D, g_cm, "g")
            def v_u(eng):
                last = None
                for i in range(KT):
                    eng.tensor_mul(wide(u_cm, i), wide(x_cm, i), bc_r[:, :])
                for i in range(KT):
                    last = eng.tensor_sub(wide(u_cm, i), wide(u_cm, i),
                                          bc_mr[:, :])
                return last
            P.step("vector", v_u, reads=["x", "bc"], writes=["u"])
            if ci == 0:
                dump("u", lambda: u_cm[:, :], ["u"])

            def win_evac(eng, oc, reg):
                if oc < NT_DIN:
                    return eng.activation(wide(z_cm, oc), reg, AF.Silu)
                elif oc < NT_DIN + NT_CONV:
                    i = oc - NT_DIN
                    return eng.activation(
                        xbc_cm[:, i * (C + 4) + 3:i * (C + 4) + 3 + C],
                        reg, AF.Copy)
                else:
                    return eng.activation(dtf[:, :], reg[0:NH, :], AF.Copy)

            win_bank = [0]
            def win_piece(p):
                slot, waits = cw(p)
                octs = ([32, 33] if p == 0 else
                        list(range((p - 1) * 4, (p - 1) * 4 + 4)))
                for half in range(len(octs) // 2):
                    ohs = octs[half * 2:half * 2 + 2]
                    ppn = ["pC", "pD"][win_bank[0] % 2]
                    pp = [pC, pD][win_bank[0] % 2]
                    win_bank[0] += 1
                    def t_win(eng, half=half, slot=slot, pp=pp):
                        last = None
                        for jj in range(2):
                            j = half * 2 + jj
                            for k in range(KT):
                                last = eng.matmul(
                                    pp[:, jj * 512:jj * 512 + 512],
                                    wring[slot][:, (j * KT + k) * 128:
                                                (j * KT + k + 1) * 128],
                                    wide(u_cm, k),
                                    start=(k == 0), stop=(k == KT - 1))
                        return last
                    P.step("tensor", t_win, reads=["u", f"wr{slot}"],
                           writes=[ppn], waits=waits if half == 0 else ())
                    wrs = sorted({"z" if oc < NT_DIN else
                                  ("xbc" if oc < NT_DIN + NT_CONV else "dtf")
                                  for oc in ohs})
                    if p >= 5:
                        wrs += ["e16a", "gw16a", "gw16b", "dsc"]
                    def s_ev(eng, ohs=ohs, pp=pp):
                        last = None
                        for jj, oc in enumerate(ohs):
                            last = win_evac(eng, oc,
                                            pp[:, jj * 512:jj * 512 + 512])
                        return last
                    P.step("scalar", s_ev, reads=[ppn, "xbc_tail"],
                           writes=wrs)
                issue()

            win_piece(0)
            def v_dt1(eng):
                eng.tensor_scalar_add(dtf[:, :], dtf[:, :], dtb_col[:, :])
                eng.tensor_scalar_mul(small[:, 0:C], dtf[:, :], -1.0)
                return eng.tensor_max(small[:, 0:C], small[:, 0:C], dtf[:, :])
            P.step("vector", v_dt1, reads=["dtf", "dtb"],
                   writes=["dtf", "small"])
            win_piece(1)
            def s_dt(eng):
                eng.activation(small[:, 0:C], small[:, 0:C], AF.Exp,
                               scale=cst[0:NH, 2:3])
                return eng.activation(small[:, 0:C], small[:, 0:C], AF.Ln,
                                      bias=cst[0:NH, 1:2])
            P.step("scalar", s_dt, reads=["small", "cst"], writes=["small"])
            win_piece(2)
            def v_dt2(eng):
                eng.tensor_scalar_max(dta[:, :], dtf[:, :], 0.0)
                eng.tensor_add(dtf[:, :], dta[:, :], small[:, 0:C])
                eng.tensor_scalar_mul(dta[:, :], dtf[:, :], acol[:, :])
                for qn in range(NSUB):
                    eng.tensor_tensor_scan(ellf[:, qn * Q:(qn + 1) * Q],
                                           dta[:, qn * Q:(qn + 1) * Q],
                                           zero32[:, :],
                                           0.0, ALU.add, ALU.add)
                eng.drain()
                for qn in range(NSUB):
                    eng.tensor_scalar(small[:, qn * Q:(qn + 1) * Q],
                                      ellf[:, qn * Q:(qn + 1) * Q],
                                      ellf[:, qn * Q + Q - 1:qn * Q + Q],
                                      -1.0, ALU.subtract, ALU.mult)
                return eng.tensor_copy(dt16_cm[:, :], dtf[:, :])
            P.step("vector", v_dt2, reads=["dtf", "small", "acol", "z32"],
                   writes=["dtf", "dta", "ellf", "small", "dt16"])
            win_piece(3)
            def s_dt2(eng):
                qdma(eng, "q_s", ell_dram.ap(), ellf[:, :])
                eng.activation(lam_cm[:, :], ellf[:, :], AF.Exp)
                return eng.activation(esc_cm[:, :], small[:, 0:C], AF.Exp)
            P.step("scalar", s_dt2, reads=["ellf", "small"],
                   writes=["lam", "esc", "ell_dram"])
            elwv = qdecl("q_s", 1)
            if ci == 0:
                dump("dt", lambda: dtf[:, :], ["dtf"])
                dump("ell", lambda: ellf[:, :], ["ellf"])
            win_piece(4)
            emit_ssd_chunk(ci)
            for p in range(5, 9):
                win_piece(p)
            if ci == 0:
                dump("zx", lambda: z_cm[:, :], ["z"])
                dump("xbc", lambda: xbc_cm[:, :].rearrange(
                    "p (i c) -> p i c", c=C + 4)[:, :, 3:3 + C], ["xbc"])

            for i in range(NT_CONV):
                ppn = ["pC", "pD"][i % 2]
                pp = [pC, pD][i % 2]
                def t_cv(eng, i=i, pp=pp):
                    last = None
                    for k in range(4):
                        last = eng.matmul(
                            pp[:, 0:512],
                            cdiag[:, (i * 4 + k) * 128:(i * 4 + k + 1) * 128],
                            xbc_cm[:, i * (C + 4) + k:i * (C + 4) + k + C],
                            start=(k == 0), stop=(k == 3))
                    return last
                P.step("tensor", t_cv, reads=["xbc", "xbc_tail", "cdiag"],
                       writes=[ppn])
                def s_cv(eng, i=i, pp=pp):
                    dst = wide(xs_cm, i) if i < NT_DIN else bcc_cm[:, :]
                    return eng.activation(dst, pp[:, 0:512], AF.Silu,
                                          bias=convb_c[:, i:i + 1])
                P.step("scalar", s_cv, reads=[ppn, "convb"],
                       writes=["xs" if i < NT_DIN else "bcc"])
            def g_ccc(eng):
                qdma(eng, "q_g", ccc_cm[:, :], bcc_cm[64:128, :])
            P.step("gpsimd", g_ccc, reads=["bcc"], writes=["ccc"])
            cccwv = qdecl("q_g", 1)
            def v_cccw(eng, cccwv=cccwv):
                eng.wait_ge(QS["q_g"], cccwv)
                return None
            P.step("vector", v_cccw, reads=["ccc"], writes=["ccc"])

            def v_tsave(eng):
                last = None
                for i in range(NT_CONV):
                    last = eng.tensor_copy(
                        tailbuf[:, i * 3:(i + 1) * 3],
                        xbc_cm[:, i * (C + 4) + C:i * (C + 4) + C + 3])
                return last
            P.step("vector", v_tsave, reads=["xbc"], writes=["tailbuf"])
            emit_ssd_g(ci)
            emit_ssd_xs(ci, 0)
            if ci == 0:
                dump("xs", lambda: xs_cm[:, :], ["xs"])
                dump("bcc", lambda: bcc_cm[:, :], ["bcc"])

            emit_ssd_early(ci, 0, elwv, xs_done=True)
            emit_ssd_late(ci, 0)
            for qn in range(NSUB):
                if qn + 1 < NSUB:
                    emit_ssd_early(ci, qn + 1, elwv)
                emit_ssd_y(ci, qn)
                if ci == 0 and qn == 0:
                    dump("hst0", lambda: h16[1][:, :], ["h16b"])
                if qn + 1 < NSUB:
                    emit_ssd_late(ci, qn + 1)

            if ci == 0:
                dump("g", lambda: g_cm[:, :], ["g"])
                dump("hst", lambda: h16[NSUB % 2][:, :],
                     ["h16a" if NSUB % 2 == 0 else "h16b"])

            bco = stats(g_cm, "g", NT_DIN, False, DIN, z_cm, "z",
                        defer_bc=True)
            for p in range(4):
                slot, waits = cw(9 + p)
                for j in range(2):
                    oc = p * 2 + j
                    ppn = ["pC", "pD"][oc % 2]
                    pp = [pC, pD][oc % 2]
                    def t_wo(eng, j=j, slot=slot, pp=pp):
                        last = None
                        for k in range(NT_DIN):
                            last = eng.matmul(
                                pp[:, 0:512],
                                wring[slot][:, (j * NT_DIN + k) * 128:
                                            (j * NT_DIN + k + 1) * 128],
                                wide(g_cm, k),
                                start=(k == 0), stop=(k == NT_DIN - 1))
                        return last
                    P.step("tensor", t_wo, reads=["g", f"wr{slot}"],
                           writes=[ppn], waits=waits if j == 0 else ())
                    if p == 0 and j == 0:
                        bco()
                    def v_wo(eng, oc=oc, pp=pp):
                        return eng.tensor_mul(wide(h1_cm, oc), pp[:, 0:512],
                                              bc_r[:, :])
                    P.step("vector", v_wo, reads=[ppn, "bc"], writes=["h1"])
                issue()
            def v_h1(eng):
                last = None
                for i in range(KT):
                    last = eng.tensor_add(wide(h1_cm, i), wide(h1_cm, i),
                                          wide(u_cm, i))
                return last
            P.step("vector", v_h1, reads=["h1", "u"], writes=["h1"])
            if ci == 0:
                dump("h1", lambda: h1_cm[:, :], ["h1"])

            stats(h1_cm, "h1", KT, False, D, moe_cm, "moe")
            def v_hn(eng):
                last = None
                for i in range(KT):
                    last = eng.tensor_mul(wide(hn_cm, i), wide(h1_cm, i),
                                          bc_r[:, :])
                return last
            P.step("vector", v_hn, reads=["h1", "bc"], writes=["hn"])

            for p in range(8):
                slot, waits = cw(13 + p)
                for j in range(4):
                    oc = p * 4 + j
                    ppn = ["pA", "pB"][oc % 2]
                    pp = [pA, pB][oc % 2]
                    def t_m1(eng, j=j, slot=slot, pp=pp):
                        last = None
                        for k in range(KT):
                            last = eng.matmul(
                                pp[:, 0:512],
                                wring[slot][:, (j * KT + k) * 128:(j * KT + k + 1) * 128],
                                wide(hn_cm, k),
                                start=(k == 0), stop=(k == KT - 1))
                        return last
                    P.step("tensor", t_m1, reads=["hn", f"wr{slot}"],
                           writes=[ppn], waits=waits if j == 0 else ())
                    def s_m1(eng, oc=oc, pp=pp):
                        dst = wide(z_cm, oc) if oc < 16 else wide(g_cm, oc - 16)
                        return eng.activation(dst, pp[:, 0:512], AF.Silu,
                                              bias=b1cols[:, oc:oc + 1])
                    P.step("scalar", s_m1, reads=[ppn, "b1c"],
                           writes=["z" if oc < 16 else "g"])
                issue()
            for grp in range(4):
                for o in range(2):
                    slot, waits = cw(21 + grp * 2 + o)
                    ppn = ["pC", "pD"][o]
                    pp = [pC, pD][o]
                    def t_m2(eng, slot=slot, pp=pp):
                        last = None
                        for k in range(32):
                            rhs = (wide(z_cm, k) if k < 16
                                   else wide(g_cm, k - 16))
                            last = eng.matmul(
                                pp[:, 0:512],
                                wring[slot][:, k * 128:(k + 1) * 128], rhs,
                                start=(k == 0), stop=(k == 31))
                        return last
                    P.step("tensor", t_m2, reads=["z", "g", f"wr{slot}"],
                           writes=[ppn], waits=waits)
                    def v_m2(eng, oc=grp * 2 + o, pp=pp):
                        return eng.scalar_tensor_tensor(
                            wide(u_cm, oc), pp[:, 0:512], b2cols[:, oc:oc + 1],
                            wide(h1_cm, oc), ALU.add, ALU.add)
                    P.step("vector", v_m2, reads=[ppn, "h1", "b2c"],
                           writes=["u"])
                issue(); issue()
            if ci == 0:
                dump("at", lambda: u_cm[:, :], ["u"])

            stats(u_cm, "u", KT, True, D, moe_cm, "moe")
            def v_x1(eng):
                last = None
                for i in range(KT):
                    eng.tensor_mul(wide(x1_cm, i), wide(u_cm, i), bc_r[:, :])
                for i in range(KT):
                    eng.tensor_sub(wide(x1_cm, i), wide(x1_cm, i), bc_mr[:, :])
                for i in range(KT):
                    last = eng.tensor_add(wide(x1_cm, i), wide(x1_cm, i),
                                          wide(x_cm, i))
                return last
            P.step("vector", v_x1, reads=["u", "x", "bc"],
                   writes=["x1", "xsT", "dtx"])
            if ci == 0:
                dump("x1", lambda: x1_cm[:, :], ["x1"])

            stats(x1_cm, "x1", KT, True, D, moe_cm, "moe")
            def v_x1n(eng):
                last = None
                for i in range(KT):
                    eng.tensor_mul(wide(hn_cm, i), wide(x1_cm, i), bc_r[:, :])
                for i in range(KT):
                    last = eng.tensor_sub(wide(hn_cm, i), wide(hn_cm, i),
                                          bc_mr[:, :])
                return last
            P.step("vector", v_x1n, reads=["x1", "bc"], writes=["hn"])
            if ci == 0:
                dump("x1n", lambda: hn_cm[:, :], ["hn"])

            def t_lg(eng):
                last = None
                for k in range(KT):
                    last = eng.matmul(pA[0:8, :], gatew_sb[:, k * 8:(k + 1) * 8],
                                      wide(hn_cm, k),
                                      start=(k == 0), stop=(k == KT - 1))
                return last
            P.step("tensor", t_lg, reads=["hn", "gw_sb"], writes=["pA"])
            def v_lg(eng):
                return eng.tensor_copy(small[0:8, 0:C], pA[0:8, :])
            P.step("vector", v_lg, reads=["pA"], writes=["small"],
                   dpre=True)
            def t_lgT(eng):
                last = None
                for tt in range(4):
                    last = eng.transpose(pB[:, tt * 8:(tt + 1) * 8],
                                         small[0:8, tt * 128:(tt + 1) * 128],
                                         i128f[0:8, 0:8])
                return last
            P.step("tensor", t_lgT, reads=["small", "i128f"], writes=["pB"])
            def v_lt(eng):
                return eng.tensor_copy(logT[:, :], pB[:, 0:32])
            P.step("vector", v_lt, reads=["pB"], writes=["logT"],
                   dpre=True)

            def T3(t):
                return t[:, :].rearrange("p (i e) -> p i e", e=NE)
            def v_r1(eng):
                return eng.reduce_max(rt_m[:, 0:4], T3(logT), axis=AX.X)
            P.step("vector", v_r1, reads=["logT"], writes=["rt_m"],
                   dpre=True)
            def v_r2(eng):
                eng.tensor_tensor(T3(rt_e), T3(logT),
                                  rt_m[:, 0:4, None].broadcast_to([128, 4, NE]),
                                  ALU.is_ge)
                eng.scalar_tensor_tensor(rt_s[:, :], rt_e[:, :], -1e30,
                                         logT[:, :], ALU.mult, ALU.add)
                return eng.reduce_max(rt_m[:, 4:8], T3(rt_s), axis=AX.X)
            P.step("vector", v_r2, reads=["logT", "rt_m"],
                   writes=["rt_m", "rt_e", "rt_s"], dpre=True)
            def v_r3(eng):
                return eng.tensor_tensor(
                    T3(rt_e), T3(logT),
                    rt_m[:, 0:4, None].broadcast_to([128, 4, NE]), ALU.subtract)
            P.step("vector", v_r3, reads=["logT", "rt_m"], writes=["rt_e"],
                   dpre=True)
            def s_r(eng):
                return eng.activation(rt_e[:, :], rt_e[:, :], AF.Exp)
            P.step("scalar", s_r, reads=["rt_e"], writes=["rt_e"])
            def v_r4(eng):
                return eng.reduce_sum(rt_m[:, 0:4], T3(rt_e), axis=AX.X)
            P.step("vector", v_r4, reads=["rt_e"], writes=["rt_m"],
                   dpre=True)
            def v_r5(eng):
                return eng.reciprocal(rt_m[:, 0:4], rt_m[:, 0:4])
            P.step("vector", v_r5, reads=["rt_m"], writes=["rt_m"],
                   dpre=True)
            def v_r6(eng):
                eng.tensor_tensor(T3(rt_s), T3(logT),
                                  rt_m[:, 4:8, None].broadcast_to([128, 4, NE]),
                                  ALU.is_ge)
                eng.tensor_mul(rt_s[:, :], rt_s[:, :], rt_e[:, :])
                return eng.tensor_tensor(
                    T3(rt_s), T3(rt_s),
                    rt_m[:, 0:4, None].broadcast_to([128, 4, NE]), ALU.mult)
            P.step("vector", v_r6, reads=["logT", "rt_m", "rt_e"],
                   writes=["rt_s"], dpre=True)
            if ci == 0:
                dump("sc", lambda: rt_s[:, :], ["rt_s"])
            def t_sT(eng):
                return eng.transpose(pB[0:32, 0:128], rt_s[:, :], i128f[:, :])
            P.step("tensor", t_sT, reads=["rt_s", "i128f"], writes=["pB"])
            def v_sT(eng):
                return eng.tensor_copy(sc_cmT[:, :], pB[0:32, 0:128])
            P.step("vector", v_sT, reads=["pB"], writes=["sc_cmT"],
                   dpre=True)
            def s_sd(eng):
                qdma(eng, "q_s", sc_dram.ap(), sc_cmT[:, :])
            P.step("scalar", s_sd, reads=["sc_cmT"], writes=["sc_dram"])
            scwv = qdecl("q_s", 1)

            def ge_tile(t):
                return wide(z_cm, t) if t < 16 else wide(g_cm, t - 16)
            for e in range(NE):
                base = 29 + e * 2
                s1, w1 = cw(base + 0)
                def g_sr(eng, e=e, scwv=scwv):
                    eng.wait_ge(QS["q_s"], scwv)
                    qdma(eng, "q_g",
                         srow[0:1, :].rearrange("p (i t) -> p i t", t=128),
                         sc_dram.ap().rearrange("(i e) t -> e i t",
                                                e=NE)[e][None, :, :])
                P.step("gpsimd", g_sr, reads=["sc_dram"], writes=["srow"])
                srwv = qdecl("q_g", 1)
                def t_sb(eng, srwv=srwv):
                    eng.wait_ge(QS["q_g"], srwv)
                    return eng.matmul(pA[:, :], ones_b[:, :], srow[0:1, :],
                                      start=True, stop=True)
                P.step("tensor", t_sb, reads=["srow", "ones"], writes=["pA"])
                def v_sb(eng):
                    return eng.tensor_copy(sbc[:, :], pA[:, :])
                P.step("vector", v_sb, reads=["pA"], writes=["h1"])

                def preg4(j):
                    pp = pC if j < 2 else pD
                    return pp[:, (j % 2) * 512:(j % 2) * 512 + 512]
                s3, w33 = cw(base + 1)
                for hf, (pp, ppn) in enumerate(((pC, "pC"), (pD, "pD"))):
                    def t_p1(eng, hf=hf, s1=s1, pp=pp):
                        last = None
                        for jj in range(2):
                            j = hf * 2 + jj
                            for k in range(KT):
                                last = eng.matmul(
                                    pp[:, jj * 512:jj * 512 + 512],
                                    wring[s1][:, (j * KT + k) * 128:
                                              (j * KT + k + 1) * 128],
                                    wide(hn_cm, k),
                                    start=(k == 0), stop=(k == KT - 1))
                        return last
                    P.step("tensor", t_p1, reads=["hn", f"wr{s1}"],
                           writes=[ppn], waits=w1 if hf == 0 else ())
                    def v_p1(eng, hf=hf, pp=pp):
                        eng.tensor_tensor(p1s[:, hf * 1024:hf * 1024 + 512],
                                          pp[:, 0:512], sbc[:, :], ALU.mult)
                        return eng.tensor_tensor(
                            p1s[:, hf * 1024 + 512:hf * 1024 + 1024],
                            pp[:, 512:1024], sbc[:, :], ALU.mult)
                    P.step("vector", v_p1, reads=[ppn, "h1"], writes=["u"])
                    def s_p1(eng, hf=hf):
                        return eng.activation(
                            p1s[:, hf * 1024:(hf + 1) * 1024],
                            p1s[:, hf * 1024:(hf + 1) * 1024], AF.Silu)
                    P.step("scalar", s_p1, reads=["u"], writes=["u"])
                issue()
                for hf, (pp, ppn) in enumerate(((pC, "pC"), (pD, "pD"))):
                    def t_p3(eng, hf=hf, s3=s3, pp=pp):
                        last = None
                        for jj in range(2):
                            j = hf * 2 + jj
                            for k in range(KT):
                                last = eng.matmul(
                                    pp[:, jj * 512:jj * 512 + 512],
                                    wring[s3][:, (j * KT + k) * 128:
                                              (j * KT + k + 1) * 128],
                                    wide(hn_cm, k),
                                    start=(k == 0), stop=(k == KT - 1))
                        return last
                    P.step("tensor", t_p3, reads=["hn", f"wr{s3}"],
                           writes=[ppn], waits=w33 if hf == 0 else ())
                    def v_p3(eng, hf=hf, pp=pp):
                        eng.tensor_tensor(p3s[:, hf * 1024:hf * 1024 + 512],
                                          pp[:, 0:512], sbc[:, :], ALU.mult)
                        return eng.tensor_tensor(
                            p3s[:, hf * 1024 + 512:hf * 1024 + 1024],
                            pp[:, 512:1024], sbc[:, :], ALU.mult)
                    P.step("vector", v_p3, reads=[ppn, "h1"], writes=["u"])
                    def g_ge(eng, e=e, hf=hf):
                        eng.tensor_mul(ge_tile(e * 4 + hf * 2),
                                       p1s[:, hf * 1024:hf * 1024 + 512],
                                       p3s[:, hf * 1024:hf * 1024 + 512])
                        return eng.tensor_mul(
                            ge_tile(e * 4 + hf * 2 + 1),
                            p1s[:, hf * 1024 + 512:hf * 1024 + 1024],
                            p3s[:, hf * 1024 + 512:hf * 1024 + 1024])
                    P.step("vector", g_ge, reads=["u"], writes=["z", "g"],
                           dpre=True)
                issue()
            for og in range(2):
                for e in range(NE):
                    s2, w2 = cw(29 + 16 + og * NE + e)
                    def t_w2(eng, e=e, og=og, s2=s2, w2=w2):
                        for qn_, v in w2:
                            eng.wait_ge(QS[qn_], v)
                        last = None
                        for oj in range(4):
                            oct = og * 4 + oj
                            pp = pC if oj < 2 else pD
                            for k in range(4):
                                last = eng.matmul(
                                    pp[:, (oj % 2) * 512:(oj % 2) * 512 + 512],
                                    wring[s2][:, (oct * 4 + k) * 128:(oct * 4 + k + 1) * 128],
                                    ge_tile(e * 4 + k),
                                    start=(e == 0 and k == 0),
                                    stop=(e == NE - 1 and k == 3))
                        return last
                    P.step("tensor", t_w2, reads=["z", "g", f"wr{s2}"],
                           writes=["pC", "pD"])
                    issue()
                def v_ym(eng, og=og):
                    last = None
                    for oj in range(4):
                        pp = pC if oj < 2 else pD
                        last = eng.tensor_copy(
                            wide(moe_cm, og * 4 + oj),
                            pp[:, (oj % 2) * 512:(oj % 2) * 512 + 512])
                    return last
                P.step("vector", v_ym, reads=["pC", "pD"], writes=["moe"])
            if ci == 0:
                dump("moe", lambda: moe_cm[:, :], ["moe"])

            stats(moe_cm, "moe", KT, True, D, z_cm, "z")
            def v_x2(eng):
                last = None
                for i in range(KT):
                    eng.tensor_mul(wide(x_cm, i), wide(moe_cm, i), bc_r[:, :])
                for i in range(KT):
                    eng.tensor_sub(wide(x_cm, i), wide(x_cm, i), bc_mr[:, :])
                for i in range(KT):
                    last = eng.tensor_add(wide(x_cm, i), wide(x_cm, i),
                                          wide(x1_cm, i))
                return last
            P.step("vector", v_x2, reads=["moe", "x1", "bc"], writes=["x"])

            for tt in range(C // 128):
                def t_ot(eng, tt=tt):
                    last = None
                    for k in range(KT):
                        last = eng.transpose(
                            pT[:, k * 128:(k + 1) * 128],
                            wide(x_cm, k, (tt * 128, tt * 128 + 128)),
                            i128b[:, :])
                    return last
                P.step("tensor", t_ot, reads=["x", "i128b"], writes=["pT"])
                def v_ot(eng, tt=tt):
                    return eng.tensor_copy(x2t32[tt % 2][:, :], pT[:, :])
                wvp = store_wv[tt % 2]
                P.step("vector", v_ot, reads=["pT"], writes=["hn"],
                       waits=([("q_s", wvp)] if wvp else ()))
                def s_ot(eng, tt=tt, tok0=tok0):
                    qdma(eng, "q_s",
                         out_t.ap()[tok0 + tt * 128:tok0 + (tt + 1) * 128, :],
                         x2t32[tt % 2][:, :])
                P.step("scalar", s_ot, reads=["hn"], writes=["outq"])
                store_wv[tt % 2] = qdecl("q_s", 1)

        def fin(eng):
            eng.wait_ge(QS["q_s"], qcnt["q_s"])
            eng.wait_ge(QS["q_w"], qcnt["q_w"])
            eng.wait_ge(QS["q_g"], qcnt["q_g"])
            return None
        P.step("scalar", fin, reads=["outq", "dbgq"], writes=[])

        ordinal = {}
        count = {e: 0 for e in ENGINES}
        for sid, s in enumerate(P.steps):
            count[s["eng"]] += 1
            ordinal[sid] = count[s["eng"]]
        for sid, s in enumerate(P.steps):
            w = {}
            deps = set(s["deps"])
            if serial and sid > 0:
                deps.add(sid - 1)
            for d in deps:
                ds = P.steps[d]
                if ds["eng"] == s["eng"]:
                    continue
                w[ds["eng"]] = max(w.get(ds["eng"], 0), ordinal[d])
            s["semwaits"] = w

        with nc.Block() as blk:
            for ename in ENGINES:
                def emit_engine(eng, ename=ename):
                    for s in P.steps:
                        if s["eng"] != ename:
                            continue
                        if s["dpre"]:
                            eng.drain()
                        for se, v in s["semwaits"].items():
                            eng.wait_ge(esem[se], v)
                        for qn_, v in s["waits"]:
                            eng.wait_ge(QS[qn_], v)
                        last = s["fn"](eng)
                        if last is None:
                            last = eng.nop()
                        last.then_inc(esem[ename], 1)
                getattr(blk, ename)(emit_engine)
        for qn_ in qcnt:
            assert qrt[qn_] == qcnt[qn_], (qn_, qrt[qn_], qcnt[qn_])
    return nc, dbg


def make_consts():
    i128 = np.eye(128, dtype=np.float32)
    m01 = np.triu(np.ones((128, 128), dtype=np.float32))
    r01 = np.zeros((NH, 4096), dtype=np.float32)
    for f in range(4096):
        h = 16 * (f // 2048) + (f % 2048) // Q
        r01[h, f] = -1.0
    slot = np.zeros((NH, 128), dtype=np.float32)
    for h in range(NH):
        for p in range(128):
            if h % 2 == p // 64:
                slot[h, p] = 1.0
    r02 = np.zeros((NH, 16), dtype=np.float32)
    for h in range(NH):
        r02[h, h // 2] = 1.0
    return dict(c_i128=i128, c_m01=m01, c_r01=r01, c_slot=slot, c_r02=r02)


def make_inmap(inputs, b, L=8192):
    m = dict(
        x=np.ascontiguousarray(inputs["L0_attn_hidden_tokens"][b, :L]),
        W_in=inputs["L0_attn_W_in"], conv_w=inputs["L0_attn_conv_w"],
        conv_b=inputs["L0_attn_conv_b"], dt_bias=inputs["L0_attn_dt_bias"],
        A_log=inputs["L0_attn_A_log"], D_param=inputs["L0_attn_D_param"],
        norm_w=inputs["L0_attn_norm_w"], W_out=inputs["L0_attn_W_out"],
        rms_w=inputs["L0_attn_rms_w"], mlp_w1=inputs["L0_attn_mlp_w1"],
        mlp_b1=inputs["L0_attn_mlp_b1"], mlp_w2=inputs["L0_attn_mlp_w2"],
        mlp_b2=inputs["L0_attn_mlp_b2"], gate_w=inputs["L0_ffn_gate_w"],
        moe_w1=inputs["L0_ffn_w1"], moe_w2=inputs["L0_ffn_w2"],
        moe_w3=inputs["L0_ffn_w3"])
    m.update(make_consts())
    return {k: np.asarray(v, dtype=np.float32) for k, v in m.items()}


def kernel(**inputs):
    L = 8192
    nc, _ = build_nc(L=L)
    in_maps = [make_inmap(inputs, b, L) for b in range(8)]
    res = run_bass_kernel_spmd(nc, in_maps, core_ids=list(range(8)))
    out = np.stack([np.asarray(r["out"]) for r in res.results], axis=0)
    return out.astype(np.float32)


# version 14
# speedup vs baseline: 1.0304x; 1.0304x over previous
import numpy as np
import ml_dtypes
from contextlib import ExitStack

import concourse.bass as bass
import concourse.mybir as mybir
from concourse.bass_utils import run_bass_kernel_spmd

F32 = mybir.dt.float32
BF16 = mybir.dt.bfloat16
AF = mybir.ActivationFunctionType
ALU = mybir.AluOpType
AX = mybir.AxisListType

D = 1024
DIN = 2048
NH = 32
HD = 64
DS = 64
CONV = 2176
DPROJ = 4256
MLP = 4096
NE = 8
EH = 512
EPS = 1e-5
Q = 128
C = 512
NSUB = C // Q

KT = D // 128
NT_DIN = DIN // 128
NT_CONV = 17
NT_PROJ = 34

ENGINES = ("tensor", "vector", "scalar", "gpsimd", "sync")


class Prog:
    def __init__(self):
        self.steps = []
        self.res = {}

    def step(self, eng, fn, reads=(), writes=(), waits=(), fence=False,
             dpre=False):
        sid = len(self.steps)
        deps = set()
        for n in reads:
            e = self.res.get(n)
            if e and e["w"] is not None:
                deps.add(e["w"])
        for n in writes:
            e = self.res.get(n)
            if e:
                if e["w"] is not None:
                    deps.add(e["w"])
                deps.update(e["r"])
        for n in reads:
            self.res.setdefault(n, {"w": None, "r": []})["r"].append(sid)
        for n in writes:
            self.res[n] = {"w": sid, "r": []}
        self.steps.append(dict(eng=eng, fn=fn, deps=deps, waits=list(waits),
                               fence=fence, dpre=dpre))
        return sid


def build_nc(L=8192, debug=(), serial=False):
    NCH = L // C
    nc = bass.Bass("TRN2", target_bir_lowering=False,
                   detect_race_conditions=False)

    x_in = nc.dram_tensor("x", [L, D], F32, kind="ExternalInput")
    w_in = nc.dram_tensor("W_in", [D, DPROJ], F32, kind="ExternalInput")
    conv_w = nc.dram_tensor("conv_w", [CONV, 4], F32, kind="ExternalInput")
    conv_b = nc.dram_tensor("conv_b", [CONV], F32, kind="ExternalInput")
    dt_bias = nc.dram_tensor("dt_bias", [NH], F32, kind="ExternalInput")
    a_log = nc.dram_tensor("A_log", [NH], F32, kind="ExternalInput")
    d_param = nc.dram_tensor("D_param", [NH], F32, kind="ExternalInput")
    norm_w = nc.dram_tensor("norm_w", [DIN], F32, kind="ExternalInput")
    w_out = nc.dram_tensor("W_out", [DIN, D], F32, kind="ExternalInput")
    rms_w = nc.dram_tensor("rms_w", [D], F32, kind="ExternalInput")
    mlp_w1 = nc.dram_tensor("mlp_w1", [D, MLP], F32, kind="ExternalInput")
    mlp_b1 = nc.dram_tensor("mlp_b1", [MLP], F32, kind="ExternalInput")
    mlp_w2 = nc.dram_tensor("mlp_w2", [MLP, D], F32, kind="ExternalInput")
    mlp_b2 = nc.dram_tensor("mlp_b2", [D], F32, kind="ExternalInput")
    gate_w = nc.dram_tensor("gate_w", [D, NE], F32, kind="ExternalInput")
    moe_w1 = nc.dram_tensor("moe_w1", [NE, D, EH], F32, kind="ExternalInput")
    moe_w2 = nc.dram_tensor("moe_w2", [NE, EH, D], F32, kind="ExternalInput")
    moe_w3 = nc.dram_tensor("moe_w3", [NE, D, EH], F32, kind="ExternalInput")
    c_i128 = nc.dram_tensor("c_i128", [128, 128], F32, kind="ExternalInput")
    c_m01 = nc.dram_tensor("c_m01", [128, 128], F32, kind="ExternalInput")
    c_r01 = nc.dram_tensor("c_r01", [NH, 4096], F32, kind="ExternalInput")
    c_slot = nc.dram_tensor("c_slot", [NH, 128], F32, kind="ExternalInput")
    c_r02 = nc.dram_tensor("c_r02", [NH, 16], F32, kind="ExternalInput")

    out_t = nc.dram_tensor("out", [L, D], F32, kind="ExternalOutput")

    dbg = {}
    for nmn, shp in [("u", [128, KT * C]), ("zx", [128, NT_DIN * C]),
                     ("xbc", [128, NT_CONV * C]), ("xs", [128, NT_DIN * C]),
                     ("bcc", [128, C]), ("dt", [NH, C]), ("ell", [NH, C]),
                     ("gw", [128, 2048]), ("xsT", [128, 2048]),
                     ("g", [128, NT_DIN * C]), ("hst", [DS, NH * HD]),
                     ("h1", [128, KT * C]), ("at", [128, KT * C]),
                     ("x1", [128, KT * C]), ("x1n", [128, KT * C]),
                     ("sc", [128, 32]), ("moe", [128, KT * C]),
                     ("dtx", [128, 2048]), ("ys0", [128, 1024]),
                     ("hst0", [DS, NH * HD]), ("g0", [128, 1024])]:
        if nmn in debug:
            dbg[nmn] = nc.dram_tensor("dbg_" + nmn, shp, F32,
                                      kind="ExternalOutput")

    x_bf = nc.dram_tensor("x_bf", [L, D], BF16, kind="Internal")
    wt_in = nc.dram_tensor("wt_in", [NT_PROJ * KT, 128, 128], BF16, kind="Internal")
    wt_out = nc.dram_tensor("wt_out", [8 * NT_DIN, 128, 128], BF16, kind="Internal")
    wt_m1 = nc.dram_tensor("wt_m1", [32 * KT, 128, 128], BF16, kind="Internal")
    wt_m2 = nc.dram_tensor("wt_m2", [8 * 32, 128, 128], BF16, kind="Internal")
    wt_e1 = nc.dram_tensor("wt_e1", [NE, 4 * KT, 128, 128], BF16, kind="Internal")
    wt_e2 = nc.dram_tensor("wt_e2", [NE, 8 * 4, 128, 128], BF16, kind="Internal")
    wt_e3 = nc.dram_tensor("wt_e3", [NE, 4 * KT, 128, 128], BF16, kind="Internal")
    ell_dram = nc.dram_tensor("ell_dram", [NH, C], F32, kind="Internal")
    sc_dram = nc.dram_tensor("sc_dram", [32, 128], F32, kind="Internal")

    P = Prog()
    with ExitStack() as st:
        def sb(name, shape, dt=BF16):
            return st.enter_context(nc.sbuf_tensor(name, shape, dt))

        i128f = sb("i128f", [128, 128], F32)
        i128b = sb("i128b", [128, 128], BF16)
        m01b = sb("m01b", [128, 128], BF16)
        ones_f = sb("ones_f", [1, 128], F32)
        ones_b = sb("ones_b", [1, 128], BF16)
        ones_bc = sb("ones_bc", [128, 1], BF16)
        cdiag = sb("cdiag", [128, NT_CONV * 4 * 128], BF16)
        convb_c = sb("convb_c", [128, NT_CONV], F32)
        cw_sb = sb("cw_sb", [128, NT_CONV * 4], F32)
        b1cols = sb("b1cols", [128, 32], F32)
        b2cols = sb("b2cols", [128, KT], F32)
        gatew_sb = sb("gatew_sb", [128, KT * NE], BF16)
        acol = sb("acol", [NH, 1], F32)
        dtb_col = sb("dtb_col", [NH, 1], F32)
        dp_col = sb("dp_col", [NH, 1], F32)
        dbc16 = sb("dbc16", [128, 16], F32)
        small = sb("small", [NH, 640], F32)
        nwcols = sb("nwcols", [128, 24], F32)
        cst = sb("cst", [128, 4], F32)

        x_cm = sb("x_cm", [128, KT * C])
        u_cm = sb("u_cm", [128, KT * C])
        z_cm = sb("z_cm", [128, NT_DIN * C])
        xbc_cm = sb("xbc_cm", [128, NT_CONV * (C + 4)])
        xs_cm = sb("xs_cm", [128, NT_DIN * C])
        bcc_cm = sb("bcc_cm", [128, C])
        ccc_cm = sb("ccc_cm", [DS, C])
        g_cm = sb("g_cm", [128, NT_DIN * C])
        h1_cm = sb("h1_cm", [128, KT * C])
        hn_cm = sb("hn_cm", [128, KT * C])
        x1_cm = sb("x1_cm", [128, KT * C])
        moe_cm = sb("moe_cm", [128, KT * C])

        dtf = sb("dtf", [NH, C], F32)
        dta = sb("dta", [NH, C], F32)
        ellf = sb("ellf", [NH, C], F32)
        zero32 = sb("zero32", [NH, Q], F32)
        lam_cm = sb("lam_cm", [NH, C])
        esc_cm = sb("esc_cm", [NH, C])
        dt16_cm = sb("dt16_cm", [NH, C])
        tailbuf = sb("tailbuf", [128, NT_CONV * 3])

        lamT = sb("lamT", [128, 32])
        lT32all = sb("lT32all", [128, 128], F32)
        escT = sb("escT", [128, 32])
        dtT = sb("dtT", [128, 32])
        dcol16 = sb("dcol16", [NH, NSUB])
        drow4 = sb("drow4", [1, 128])
        dexp = sb("dexp", [1, 2048])
        gtm4 = sb("gtm4", [128, NSUB * Q])
        bcT = sb("bcT", [128, 64])
        h16 = [sb("h16a", [DS, NH * HD]), sb("h16b", [DS, NH * HD])]
        e16h = [xbc_cm[:, 0:2048], h1_cm[:, 2048:4096]]
        gw16a = xbc_cm[:, 2048:4096]
        gw16b = xbc_cm[:, 4096:6144]
        dsc32 = xbc_cm[:, 6144:8192].bitcast(F32)
        xsT = x1_cm[:, 0:2048]
        dtx = x1_cm[:, 2048:4096]
        dtxs = moe_cm[:, 0:2048]
        ya16 = moe_cm[:, 2048:3072]
        ystm = moe_cm[:, 3072:4096]
        slsb = h1_cm[0:64, 0:1024]

        r_m = sb("r_m", [1, C], F32)
        r_v = sb("r_v", [1, C], F32)
        r_t = sb("r_t", [1, C], F32)
        r_rstd = sb("r_rstd", [1, C])
        r_mr = sb("r_mr", [1, C])
        bc_r = sb("bc_r", [128, C])
        bc_mr = sb("bc_mr", [128, C])

        logT = sb("logT", [128, 32], F32)
        rt_m = sb("rt_m", [128, 8], F32)
        rt_e = sb("rt_e", [128, 32], F32)
        rt_s = sb("rt_s", [128, 32], F32)
        sc_cmT = sb("sc_cmT", [32, 128], F32)
        srow = sb("srow", [1, C])
        sbc = h1_cm[:, 0:512]
        p1s = u_cm[:, 0:2048]
        p3s = u_cm[:, 2048:4096]
        x2t32 = [hn_cm[:, 0:2048].bitcast(F32),
                 hn_cm[:, 2048:4096].bitcast(F32)]

        NRING = 4
        wring = [sb(f"wring{i}", [128, 4096]) for i in range(NRING)]

        pA = st.enter_context(nc.psum_tensor("pA", [128, 512], F32))
        pB = st.enter_context(nc.psum_tensor("pB", [128, 512], F32))
        pC = st.enter_context(nc.psum_tensor("pC", [128, 1024], F32))
        pD = st.enter_context(nc.psum_tensor("pD", [128, 1024], F32))
        pT = st.enter_context(nc.psum_tensor("pT", [128, 1024], BF16))
        pU = st.enter_context(nc.psum_tensor("pU", [128, 1024], BF16))

        esem = {e: st.enter_context(nc.semaphore("sem_" + e)) for e in ENGINES}
        q_w = st.enter_context(nc.semaphore("q_w"))
        q_s = st.enter_context(nc.semaphore("q_s"))
        q_g = st.enter_context(nc.semaphore("q_g"))
        QS = {"q_w": q_w, "q_s": q_s, "q_g": q_g}
        qcnt = {"q_w": 0, "q_g": 0, "q_s": 0}
        qrt = {"q_w": 0, "q_g": 0, "q_s": 0}

        def qdma(eng, qname, out, in_, **kw):
            eng.dma_start(out, in_, **kw).then_inc(QS[qname], 16)
            qrt[qname] += 16
            return qrt[qname]

        def qdecl(qname, n=1):
            pre = qcnt[qname]
            P.steps[-1]["waits"].append((qname, pre))
            qcnt[qname] += 16 * n
            return qcnt[qname]

        def wide(t, i, cols=None):
            a, b = cols if cols else (0, C)
            return t[:, i * C + a:i * C + b]

        def w3(t, tiles, cols):
            a, b = tiles
            c0, c1 = cols
            return t[:, :].rearrange("p (i c) -> p i c", c=C)[:, a:b, c0:c1]

        def pro_g(eng):
            qdma(eng, "q_g", i128b[:, :], c_i128.ap())
            qdma(eng, "q_g", m01b[:, :], c_m01.ap())
            qdma(eng, "q_g", cw_sb[:, :].rearrange("p (i k) -> p i k", k=4),
                 conv_w.ap().rearrange("(i p) k -> p i k", p=128))
            qdma(eng, "q_g", convb_c[:, :],
                 conv_b.ap().rearrange("(i p) -> p i", p=128),
                 allow_slow_non_contiguous=True)
            qdma(eng, "q_g", b1cols[:, :],
                 mlp_b1.ap().rearrange("(i p) -> p i", p=128),
                 allow_slow_non_contiguous=True)
            qdma(eng, "q_g", b2cols[:, :],
                 mlp_b2.ap().rearrange("(i p) -> p i", p=128),
                 allow_slow_non_contiguous=True)
            qdma(eng, "q_g", gatew_sb[:, :].rearrange("p (k e) -> p k e", e=NE),
                 gate_w.ap().rearrange("(k p) e -> p k e", p=128))
            qdma(eng, "q_g", x_bf.ap(), x_in.ap())
        P.step("gpsimd", pro_g, writes=["i128b", "m01b", "cw", "convb", "b1c",
                                        "b2c", "gw_sb", "x_bf"])
        PRO_G1 = qdecl("q_g", 8)

        def pro_s(eng):
            qdma(eng, "q_s", i128f[:, :], c_i128.ap())
            qdma(eng, "q_s", acol[:, :], a_log.ap()[:, None])
            qdma(eng, "q_s", dtb_col[:, :], dt_bias.ap()[:, None])
            qdma(eng, "q_s", dp_col[:, :], d_param.ap()[:, None])
            qdma(eng, "q_s", small[:, 0:128], c_slot.ap())
            qdma(eng, "q_s", small[:, 128:144], c_r02.ap())
            qdma(eng, "q_s", nwcols[:, 0:16],
                 norm_w.ap().rearrange("(i p) -> p i", p=128),
                 allow_slow_non_contiguous=True)
            qdma(eng, "q_s", nwcols[:, 16:24],
                 rms_w.ap().rearrange("(i p) -> p i", p=128),
                 allow_slow_non_contiguous=True)
        P.step("scalar", pro_s, writes=["i128f", "acol", "dtb", "dp",
                                        "small", "nwcols"])
        PRO_S1 = qdecl("q_s", 8)

        def tile_casts(eng):
            for oc in range(NT_PROJ):
                ocw = 128 if oc < NT_PROJ - 1 else 32
                qdma(eng, "q_g", wt_in.ap()[oc * KT:(oc + 1) * KT, :, 0:ocw],
                     w_in.ap()[:, oc * 128:oc * 128 + ocw]
                     .rearrange("(k p) c -> k p c", p=128))
            for oc in range(8):
                for kg in range(2):
                    qdma(eng, "q_g",
                         wt_out.ap()[oc * NT_DIN + kg * 8:oc * NT_DIN + kg * 8 + 8],
                         w_out.ap()[kg * 1024:(kg + 1) * 1024,
                                    oc * 128:(oc + 1) * 128]
                         .rearrange("(k p) c -> k p c", p=128))
            for oc in range(32):
                qdma(eng, "q_g", wt_m1.ap()[oc * KT:(oc + 1) * KT],
                     mlp_w1.ap()[:, oc * 128:(oc + 1) * 128]
                     .rearrange("(k p) c -> k p c", p=128))
            for oc in range(8):
                for kg in range(4):
                    qdma(eng, "q_g",
                         wt_m2.ap()[oc * 32 + kg * 8:oc * 32 + kg * 8 + 8],
                         mlp_w2.ap()[kg * 1024:(kg + 1) * 1024,
                                     oc * 128:(oc + 1) * 128]
                         .rearrange("(k p) c -> k p c", p=128))
            for e in range(NE):
                for oc in range(4):
                    qdma(eng, "q_g", wt_e1.ap()[e, oc * KT:(oc + 1) * KT],
                         moe_w1.ap()[e][:, oc * 128:(oc + 1) * 128]
                         .rearrange("(k p) c -> k p c", p=128))
                    qdma(eng, "q_g", wt_e3.ap()[e, oc * KT:(oc + 1) * KT],
                         moe_w3.ap()[e][:, oc * 128:(oc + 1) * 128]
                         .rearrange("(k p) c -> k p c", p=128))
                for oc in range(8):
                    qdma(eng, "q_g", wt_e2.ap()[e, oc * 4:(oc + 1) * 4],
                         moe_w2.ap()[e][:, oc * 128:(oc + 1) * 128]
                         .rearrange("(k p) c -> k p c", p=128))
        P.step("gpsimd", tile_casts, writes=["wt_all"])
        WT_DONE = qdecl("q_g", 242)

        def v_init(eng):
            eng.memset(ones_f[:, :], 1.0)
            eng.memset(ones_b[:, :], 1.0)
            eng.memset(ones_bc[:, :], 1.0)
            eng.memset(h16[0][:, :], 0.0)
            eng.memset(h16[1][:, :], 0.0)
            eng.memset(tailbuf[:, :], 0.0)
            eng.memset(cst[:, 0:1], EPS)
            eng.memset(cst[:, 1:2], 1.0)
            eng.memset(cst[:, 2:3], -1.0)
            eng.memset(cst[:, 3:4], -0.5)
            eng.memset(zero32[:, :], 0.0)
            return eng.memset(wring[0][:, :], 0.0)
        P.step("vector", v_init, writes=["ones", "h16a", "h16b", "tailbuf",
                                         "z32", "wr0"])

        def g_pad(eng):
            qdma(eng, "q_g",
                 wt_in.ap()[(NT_PROJ - 1) * KT:NT_PROJ * KT, :, 32:128]
                 .rearrange("k p c -> p k c"),
                 wring[0][:, 0:KT * 96].rearrange("p (k c) -> p k c", c=96))
        P.step("gpsimd", g_pad, reads=["wr0", "wt_all"], writes=["wt_pad"],
               waits=[("q_g", WT_DONE)])
        PAD_DONE = qdecl("q_g", 1)

        def s_acol(eng):
            return eng.activation(acol[:, :], acol[:, :], AF.Exp)
        P.step("scalar", s_acol, reads=["acol"], writes=["acol"],
               waits=[("q_s", PRO_S1)])

        def v_der(eng):
            eng.tensor_scalar_mul(acol[:, :], acol[:, :], -1.0)
            for i in range(NT_CONV):
                for k in range(4):
                    eng.tensor_scalar_mul(
                        cdiag[:, (i * 4 + k) * 128:(i * 4 + k + 1) * 128],
                        i128b[:, :], cw_sb[:, i * 4 + k:i * 4 + k + 1])
            return eng.tensor_scalar_mul(small[:, 128:144], small[:, 128:144],
                                         dp_col[:, :])
        P.step("vector", v_der, reads=["acol", "i128b", "cw", "dp", "small"],
               writes=["acol", "cdiag", "small"],
               waits=[("q_g", PRO_G1), ("q_s", PRO_S1)])

        def t_dbc(eng):
            return eng.matmul(pA[:, 0:16], small[:, 0:128], small[:, 128:144],
                              start=True, stop=True)
        P.step("tensor", t_dbc, reads=["small"], writes=["pA"])

        def v_dbc(eng):
            return eng.tensor_copy(dbc16[:, :], pA[:, 0:16])
        P.step("vector", v_dbc, reads=["pA"], writes=["dbc16"])

        for r in range(NT_DIN):
            def g_l(eng, r=r):
                qdma(eng, "q_g",
                     wring[r % 2][:, 0:1024].rearrange("p (o c) -> p o c", c=128),
                     wt_out.ap().rearrange("(o k) p c -> k p o c", k=NT_DIN)[r])
            P.step("gpsimd", g_l, reads=["wt_all"], writes=[f"wr{r % 2}"],
                   waits=[("q_g", WT_DONE)])
            wv = qdecl("q_g", 1)
            def v_s(eng, r=r):
                return eng.tensor_scalar_mul(wring[r % 2][:, 0:1024],
                                             wring[r % 2][:, 0:1024],
                                             nwcols[:, r:r + 1])
            P.step("vector", v_s, reads=[f"wr{r % 2}", "nwcols"],
                   writes=[f"wr{r % 2}"], waits=[("q_g", wv)])
            def g_s(eng, r=r):
                qdma(eng, "q_g",
                     wt_out.ap().rearrange("(o k) p c -> k p o c", k=NT_DIN)[r],
                     wring[r % 2][:, 0:1024].rearrange("p (o c) -> p o c", c=128))
            P.step("gpsimd", g_s, reads=[f"wr{r % 2}"], writes=["wt_all"])
            qdecl("q_g", 1)
        for r in range(KT):
            def g_l(eng, r=r):
                qdma(eng, "q_g",
                     wring[r % 2][:, 0:4096].rearrange("p (o c) -> p o c", c=128),
                     wt_m1.ap().rearrange("(o k) p c -> k p o c", k=KT)[r])
            P.step("gpsimd", g_l, reads=["wt_all"], writes=[f"wr{r % 2}"])
            wv = qdecl("q_g", 1)
            def v_s(eng, r=r):
                return eng.tensor_scalar_mul(wring[r % 2][:, 0:4096],
                                             wring[r % 2][:, 0:4096],
                                             nwcols[:, 16 + r:17 + r])
            P.step("vector", v_s, reads=[f"wr{r % 2}", "nwcols"],
                   writes=[f"wr{r % 2}"], waits=[("q_g", wv)])
            def g_s(eng, r=r):
                qdma(eng, "q_g",
                     wt_m1.ap().rearrange("(o k) p c -> k p o c", k=KT)[r],
                     wring[r % 2][:, 0:4096].rearrange("p (o c) -> p o c", c=128))
            P.step("gpsimd", g_s, reads=[f"wr{r % 2}"], writes=["wt_all"])
            qdecl("q_g", 1)
        PRO_GD = qcnt["q_g"]

        ring_i = [0]

        def ring_load(srcs):
            slot = ring_i[0] % NRING
            qsel = "q_w" if ring_i[0] % 2 == 0 else "q_g"
            ring_i[0] += 1
            def fn(eng, srcs=srcs, slot=slot, qsel=qsel):
                for ncols, src in srcs:
                    qdma(eng, qsel,
                         wring[slot][:, 0:ncols].rearrange("p (t c) -> p t c",
                                                           c=128),
                         src.rearrange("t p c -> p t c"))
            P.step("sync" if qsel == "q_w" else "gpsimd", fn,
                   reads=["wt_all"], writes=[f"wr{slot}"],
                   waits=[("q_g", PRO_GD)])
            return slot, (qsel, qdecl(qsel, len(srcs)))

        def chunk_pieces():
            ps = []
            ps.append([(2 * KT * 128, wt_in.ap()[32 * KT:34 * KT])])
            for p in range(8):
                ps.append([(4 * KT * 128,
                            wt_in.ap()[p * 4 * KT:(p * 4 + 4) * KT])])
            for p in range(4):
                ps.append([(2 * NT_DIN * 128,
                            wt_out.ap()[p * 2 * NT_DIN:(p + 1) * 2 * NT_DIN])])
            for p in range(8):
                ps.append([(4 * KT * 128, wt_m1.ap()[p * 4 * KT:(p + 1) * 4 * KT])])
            for p in range(8):
                ps.append([(32 * 128, wt_m2.ap()[p * 32:(p + 1) * 32])])
            for e in range(NE):
                ps.append([(32 * 128, wt_e1.ap()[e])])
                ps.append([(32 * 128, wt_e3.ap()[e])])
            for og in range(2):
                for e in range(NE):
                    ps.append([(32 * 128, wt_e2.ap()[e])])
            return ps

        def stats(src, srcn, ntiles, with_mean, nfeat, sq_dst, sqn,
                  defer_bc=False, sq_eng="vector", pre_done=False):
            def v_sq(eng):
                last = None
                for i in range(ntiles):
                    last = eng.tensor_mul(wide(sq_dst, i), wide(src, i),
                                          wide(src, i))
                return last
            if not pre_done:
                P.step(sq_eng, v_sq, reads=[srcn], writes=[sqn])

            def t_mm(eng):
                last = None
                if with_mean:
                    for i in range(ntiles):
                        last = eng.matmul(pA[0:1, :], ones_bc[:, :],
                                          wide(src, i), start=(i == 0),
                                          stop=(i == ntiles - 1))
                for i in range(ntiles):
                    last = eng.matmul(pB[0:1, :], ones_bc[:, :],
                                      wide(sq_dst, i), start=(i == 0),
                                      stop=(i == ntiles - 1))
                return last
            if not pre_done:
                P.step("tensor", t_mm, reads=[srcn, sqn, "ones"],
                       writes=["pA", "pB"])

            def v_row(eng):
                if with_mean:
                    eng.tensor_scalar_mul(r_m[:, :], pA[0:1, :], 1.0 / nfeat)
                    eng.tensor_mul(r_t[:, :], r_m[:, :], r_m[:, :])
                    return eng.scalar_tensor_tensor(r_v[:, :], pB[0:1, :],
                                                    1.0 / nfeat, r_t[:, :],
                                                    ALU.mult, ALU.subtract)
                return eng.tensor_scalar_mul(r_v[:, :], pB[0:1, :], 1.0 / nfeat)
            P.step("vector", v_row, reads=["pA", "pB"], writes=["rows"])

            def s_row(eng):
                eng.activation(r_t[:, :], r_v[:, :], AF.Ln, bias=cst[0:1, 0:1])
                return eng.activation(r_rstd[:, :], r_t[:, :], AF.Exp,
                                      scale=cst[0:1, 3:4])
            P.step("scalar", s_row, reads=["rows", "cst"], writes=["rows"])

            if with_mean:
                def v_mr(eng):
                    return eng.tensor_mul(r_mr[:, :], r_m[:, :], r_rstd[:, :])
                P.step("vector", v_mr, reads=["rows"], writes=["rows"])

            def bc_part():
                def t_bc(eng):
                    last = eng.matmul(pA[:, :], ones_b[:, :], r_rstd[:, :],
                                      start=True, stop=True)
                    if with_mean:
                        last = eng.matmul(pB[:, :], ones_b[:, :], r_mr[:, :],
                                          start=True, stop=True)
                    return last
                P.step("tensor", t_bc, reads=["rows", "ones"],
                       writes=["pA", "pB"])

                def v_bc(eng):
                    last = eng.tensor_copy(bc_r[:, :], pA[:, :])
                    if with_mean:
                        last = eng.tensor_copy(bc_mr[:, :], pB[:, :])
                    return last
                P.step("vector", v_bc, reads=["pA", "pB"], writes=["bc"])
            if defer_bc:
                return bc_part
            bc_part()

        def dump(name, ap_fn, reads):
            if name not in dbg:
                return
            def fn(eng, name=name):
                v = qdma(eng, "q_g", dbg[name].ap(), ap_fn())
                eng.wait_ge(QS["q_g"], v)
            P.step("gpsimd", fn, reads=list(reads), writes=["dbgq"])
            qdecl("q_g", 1)

        def emit_ssd_g(ci):
            def t_g4(eng):
                last = None
                for q in range(NSUB):
                    last = eng.matmul(pA[:, q * Q:(q + 1) * Q],
                                      bcc_cm[0:64, q * Q:(q + 1) * Q],
                                      ccc_cm[:, q * Q:(q + 1) * Q],
                                      start=True, stop=True)
                return last
            P.step("tensor", t_g4, reads=["bcc", "ccc"], writes=["pA"])

            def v_g4(eng):
                return eng.tensor_tensor(
                    gtm4[:, :].rearrange("p (q t) -> p q t", t=Q),
                    pA[:, :].rearrange("p (q t) -> p q t", t=Q),
                    m01b[:, None, :].broadcast_to([128, NSUB, Q]), ALU.mult)
            P.step("vector", v_g4, reads=["pA", "m01b"], writes=["gtm"])

        def emit_ssd_chunk(ci):
            def t_ltall(eng):
                last = None
                for q in range(NSUB):
                    last = eng.transpose(pB[:, 32 * q:32 * (q + 1)],
                                         ellf[:, q * Q:(q + 1) * Q],
                                         i128f[0:32, 0:32])
                return last
            P.step("tensor", t_ltall, reads=["ellf", "i128f"], writes=["pB"])

            def v_ltall(eng):
                return eng.tensor_copy(lT32all[:, :], pB[:, 0:128])
            P.step("vector", v_ltall, reads=["pB"], writes=["lT32"])

            def v_dc(eng):
                return eng.tensor_copy(
                    dcol16[:, :, None],
                    lam_cm[:, :].rearrange("p (q t) -> p q t", t=Q)
                    [:, :, Q - 1:Q])
            P.step("vector", v_dc, reads=["lam"], writes=["dcol"])

            def t_dr4(eng):
                last = None
                for q in range(NSUB):
                    last = eng.transpose(pU[0:1, 32 * q:32 * (q + 1)],
                                         dcol16[:, q:q + 1],
                                         i128b[0:32, 0:32])
                return last
            P.step("tensor", t_dr4, reads=["dcol", "i128b"], writes=["pU"])

            def v_dr4(eng):
                return eng.tensor_copy(drow4[:, :], pU[0:1, 0:128])
            P.step("vector", v_dr4, reads=["pU"], writes=["drow"])

        def emit_ssd_xs(ci, qn):
            qs = (qn * Q, qn * Q + Q)

            def t_w1(eng):
                last = None
                for i in range(NT_DIN):
                    pp = pT if i < 8 else pU
                    last = eng.transpose(pp[:, (i % 8) * 128:(i % 8 + 1) * 128],
                                         wide(xs_cm, i, qs), i128b[:, :])
                return last
            P.step("tensor", t_w1, reads=["xs", "i128b"], writes=["pT", "pU"])

            def s_w1(eng):
                eng.activation(xsT[:, 0:1024], pT[:, :], AF.Copy)
                return eng.activation(xsT[:, 1024:2048], pU[:, :], AF.Copy)
            P.step("scalar", s_w1, reads=["pT", "pU"], writes=["xsT"])

        def emit_ssd_early(ci, qn, elwv, xs_done=False):
            qs = (qn * Q, qn * Q + Q)
            if not xs_done:
                emit_ssd_xs(ci, qn)

            def t_w2(eng):
                eng.transpose(pT[:, 0:64], bcc_cm[0:64, qs[0]:qs[1]],
                              i128b[0:64, 0:64])
                eng.transpose(pT[:, 64:96], lam_cm[:, qs[0]:qs[1]],
                              i128b[0:32, 0:32])
                eng.transpose(pT[:, 96:128], esc_cm[:, qs[0]:qs[1]],
                              i128b[0:32, 0:32])
                return eng.transpose(pT[:, 128:160], dt16_cm[:, qs[0]:qs[1]],
                                     i128b[0:32, 0:32])
            P.step("tensor", t_w2, reads=["bcc", "lam", "esc", "dt16", "i128b"],
                   writes=["pT"])

            for hb in range(2):
                for qq in range(2):
                    def s_lb(eng, hb=hb, qq=qq):
                        eng.wait_ge(QS["q_s"], elwv)
                        qdma(eng, "q_s",
                             dsc32[:, :].rearrange("p (h t) -> p h t", t=Q),
                             ell_dram.ap()[hb * 16 + qq * 8:hb * 16 + (qq + 1) * 8,
                                           qs[0]:qs[1]][None, :, :]
                             .broadcast_to([128, 8, Q]))
                    anchor = ["xbc"] if (qn == 0 and hb == 0 and qq == 0) else []
                    P.step("scalar", s_lb, reads=["ell_dram"],
                           writes=["dsc"] + anchor)
                    lbwv = qdecl("q_s", 1)
                    def v_d(eng, qn=qn, hb=hb, qq=qq, lbwv=lbwv):
                        eng.wait_ge(QS["q_s"], lbwv)
                        return eng.tensor_tensor(
                            dsc32[:, :].rearrange("p (h t) -> p h t", t=Q),
                            dsc32[:, :].rearrange("p (h t) -> p h t", t=Q),
                            lT32all[:, 32 * qn + 16 * hb + 8 * qq:
                                    32 * qn + 16 * hb + 8 * (qq + 1), None]
                            .broadcast_to([128, 8, Q]), ALU.subtract)
                    P.step("vector", v_d, reads=["dsc", "lT32"],
                           writes=["dsc"],
                           dpre=(qn == 0 and hb == 0 and qq == 0))
                    def s_exp(eng, hb=hb, qq=qq):
                        return eng.activation(
                            e16h[hb][:, qq * 1024:(qq + 1) * 1024],
                            dsc32[:, :], AF.Exp)
                    P.step("scalar", s_exp, reads=["dsc"],
                           writes=(["e16a"] if hb == 0 else ["e16b", "h1"]))

        def emit_ssd_late(ci, qn):
            qs = (qn * Q, qn * Q + Q)
            def s_w2(eng):
                eng.activation(bcT[:, :], pT[:, 0:64], AF.Copy)
                eng.activation(lamT[:, :], pT[:, 64:96], AF.Copy)
                eng.activation(escT[:, :], pT[:, 96:128], AF.Copy)
                return eng.activation(dtT[:, :], pT[:, 128:160], AF.Copy)
            P.step("scalar", s_w2, reads=["pT"],
                   writes=["bcT", "lamT", "escT", "dtT"])
            def v_w2(eng):
                eng.tensor_tensor(dtx[:, :].rearrange("p (h v) -> p h v", v=HD),
                                  xsT[:, :].rearrange("p (h v) -> p h v", v=HD),
                                  dtT[:, :, None].broadcast_to([128, NH, HD]),
                                  ALU.mult)
                return eng.tensor_tensor(
                    dtxs[:, :].rearrange("p (h v) -> p h v", v=HD),
                    dtx[:, :].rearrange("p (h v) -> p h v", v=HD),
                    escT[:, :, None].broadcast_to([128, NH, HD]), ALU.mult)
            P.step("vector", v_w2, reads=["dtT", "escT", "xsT"],
                   writes=["dtx", "moe"])
            if ci == 0 and qn == 0:
                dump("dtx", lambda: dtx[:, :], ["dtx"])
            for hb, gwd in ((0, gw16a), (1, gw16b)):
                def v_gw(eng, qn=qn, hb=hb, gwd=gwd):
                    return eng.scalar_tensor_tensor(
                        gwd[:, :].rearrange("p (h t) -> p h t", t=Q),
                        e16h[hb][:, :].rearrange("p (h t) -> p h t", t=Q),
                        1.0,
                        gtm4[:, qn * Q:(qn + 1) * Q][:, None, :]
                        .broadcast_to([128, 16, Q]), ALU.min, ALU.mult)
                P.step("vector", v_gw,
                       reads=["e16a" if hb == 0 else "e16b", "gtm"],
                       writes=["gw16a" if hb == 0 else "gw16b"],
                       dpre=(qn == 0 and hb == 0))
            def v_dx(eng, qn=qn):
                return eng.tensor_copy(
                    dexp[0:1, :].rearrange("p (h v) -> p h v", v=HD),
                    drow4[0:1, 32 * qn:32 * qn + 32, None]
                    .broadcast_to([1, 32, HD]))
            P.step("vector", v_dx, reads=["drow"], writes=["dexp"],
                   dpre=(qn == 0))
            if ci == 0 and qn == 0:
                dump("gw", lambda: gw16a[:, :], ["gw16a"])
                dump("xsT", lambda: xsT[:, :], ["xsT"])

        def emit_ssd_y(ci, qn):
            qs = (qn * Q, qn * Q + Q)
            hold, hnew = h16[qn % 2], h16[(qn + 1) % 2]
            holdn = "h16a" if qn % 2 == 0 else "h16b"
            hnewn = "h16b" if qn % 2 == 0 else "h16a"
            for hb, gwd in ((0, gw16a), (1, gw16b)):
                def t_yi(eng, hb=hb, gwd=gwd):
                    last = None
                    for hl in range(16):
                        h = 16 * hb + hl
                        pp = pC if hl % 2 == 0 else pD
                        last = eng.matmul(
                            pp[0:64, (hl // 2) * 128:(hl // 2 + 1) * 128],
                            dtx[:, HD * h:HD * h + HD],
                            gwd[:, Q * hl:Q * (hl + 1)],
                            start=(hl in (0, 1, 8, 9)), stop=False)
                    return last
                P.step("tensor", t_yi,
                       reads=["dtx", "gw16a" if hb == 0 else "gw16b"],
                       writes=["pC", "pD"])

                def t_ys(eng, hb=hb, hold=hold):
                    eng.matmul(pA[:, :], ccc_cm[:, qs[0]:qs[1]],
                               hold[:, hb * 1024:hb * 1024 + 512],
                               start=True, stop=True)
                    return eng.matmul(pB[:, :], ccc_cm[:, qs[0]:qs[1]],
                                      hold[:, hb * 1024 + 512:
                                           hb * 1024 + 1024],
                                      start=True, stop=True)
                P.step("tensor", t_ys, reads=["ccc", holdn],
                       writes=["pA", "pB"])

                def v_ys(eng, hb=hb):
                    eng.tensor_tensor(
                        ystm[:, 0:512].rearrange("p (h v) -> p h v", v=HD),
                        pA[:, :].rearrange("p (h v) -> p h v", v=HD),
                        lamT[:, 16 * hb:16 * hb + 8, None]
                        .broadcast_to([128, 8, HD]), ALU.mult)
                    return eng.tensor_tensor(
                        ystm[:, 512:1024].rearrange("p (h v) -> p h v", v=HD),
                        pB[:, :].rearrange("p (h v) -> p h v", v=HD),
                        lamT[:, 16 * hb + 8:16 * hb + 16, None]
                        .broadcast_to([128, 8, HD]), ALU.mult)
                P.step("vector", v_ys, reads=["pA", "pB", "lamT"],
                       writes=["moe"])
                if ci == 0 and qn == 0 and hb == 0:
                    dump("ys0", lambda: ystm[:, :], ["moe"])

                def t_yt(eng, hb=hb):
                    last = None
                    for hl in range(16):
                        pp = pC if hl % 2 == 0 else pD
                        last = eng.matmul(
                            pp[0:64, (hl // 2) * 128:(hl // 2 + 1) * 128],
                            ystm[:, HD * hl:HD * hl + HD], i128b[:, :],
                            start=False, stop=(hl in (6, 7, 14, 15)))
                    return last
                P.step("tensor", t_yt, reads=["moe", "i128b"],
                       writes=["pC", "pD"])

                def t_sl(eng, hb=hb):
                    eng.matmul(pA[0:64, :], bcT[:, :],
                               dtxs[:, hb * 1024:hb * 1024 + 512],
                               start=True, stop=True)
                    return eng.matmul(pB[0:64, :], bcT[:, :],
                                      dtxs[:, hb * 1024 + 512:
                                           hb * 1024 + 1024],
                                      start=True, stop=True)
                P.step("tensor", t_sl, reads=["bcT", "moe"],
                       writes=["pA", "pB"])

                def v_as1(eng, qn=qn, hb=hb):
                    eng.tensor_tensor(
                        ya16[:, :].rearrange("p (i t) -> p i t", t=Q),
                        w3(xs_cm, (8 * hb, 8 * hb + 8), qs),
                        dbc16[:, 8 * hb:8 * hb + 8, None]
                        .broadcast_to([128, 8, Q]), ALU.mult)
                    eng.tensor_add(ya16[0:64, :], ya16[0:64, :], pC[0:64, :])
                    return eng.tensor_add(ya16[64:128, :], ya16[64:128, :],
                                          pD[0:64, :])
                P.step("vector", v_as1, reads=["pC", "pD", "xs", "dbc16"],
                       writes=["moe"])
                if ci == 0 and qn == 0 and hb == 0:
                    dump("g0", lambda: ya16[:, :], ["moe"])

                def t_dm(eng, hb=hb):
                    pp = pC if hb == 0 else pD
                    eng.matmul(pp[0:64, 0:512], ones_b[0:1, 0:64],
                               dexp[0:1, hb * 1024:hb * 1024 + 512],
                               start=True, stop=True)
                    return eng.matmul(pp[0:64, 512:1024], ones_b[0:1, 0:64],
                                      dexp[0:1, hb * 1024 + 512:
                                           hb * 1024 + 1024],
                                      start=True, stop=True)
                P.step("tensor", t_dm, reads=["dexp", "ones"],
                       writes=["pC" if hb == 0 else "pD"])

                def v_hu(eng, hb=hb, hold=hold, hnew=hnew):
                    pp = pC if hb == 0 else pD
                    eng.tensor_tensor(ystm[0:64, :],
                                      hold[:, hb * 1024:(hb + 1) * 1024],
                                      pp[0:64, :], ALU.mult)
                    eng.tensor_add(hnew[:, hb * 1024:hb * 1024 + 512],
                                   ystm[0:64, 0:512], pA[0:64, :])
                    return eng.tensor_add(hnew[:, hb * 1024 + 512:
                                               hb * 1024 + 1024],
                                          ystm[0:64, 512:1024], pB[0:64, :])
                P.step("vector", v_hu,
                       reads=["pA", "pB", holdn, "pC" if hb == 0 else "pD"],
                       writes=[hnewn, "moe"])

                def v_as2(eng, qn=qn, hb=hb):
                    return eng.tensor_tensor(
                        w3(g_cm, (8 * hb, 8 * hb + 8), qs),
                        ya16[:, :].rearrange("p (i t) -> p i t", t=Q),
                        w3(z_cm, (8 * hb, 8 * hb + 8), qs), ALU.mult)
                P.step("vector", v_as2, reads=["moe", "z"], writes=["g"])

        store_wv = {0: 0, 1: 0}
        for ci in range(NCH):
            tok0 = ci * C
            pieces = chunk_pieces()
            pinfo = {}
            nxt = [0]

            def issue():
                if nxt[0] < len(pieces):
                    pinfo[nxt[0]] = ring_load(pieces[nxt[0]])
                    nxt[0] += 1
            for _ in range(NRING):
                issue()

            def cw(idx):
                slot, (qn_, wv) = pinfo[idx]
                return slot, [(qn_, wv)]

            def s_x(eng, tok0=tok0):
                for k in range(KT):
                    qdma(eng, "q_s", wide(x_cm, k),
                         x_bf.ap()[tok0:tok0 + C, k * 128:(k + 1) * 128],
                         transpose=True)
            P.step("scalar", s_x, reads=["x_bf"], writes=["x"],
                   waits=[("q_g", PRO_G1)])
            xwv = qdecl("q_s", KT)

            def v_tail(eng):
                last = None
                for i in range(NT_CONV):
                    last = eng.tensor_copy(
                        xbc_cm[:, i * (C + 4):i * (C + 4) + 3],
                        tailbuf[:, i * 3:(i + 1) * 3])
                return last
            P.step("vector", v_tail, reads=["tailbuf", "xbc"],
                   writes=["xbc_tail"])

            def v_xw(eng, xwv=xwv):
                eng.wait_ge(QS["q_s"], xwv)
                return None
            P.step("vector", v_xw, reads=["x"], writes=["x"])

            stats(x_cm, "x", KT, True, D, g_cm, "g")
            def v_u(eng):
                last = None
                for i in range(KT):
                    eng.tensor_mul(wide(u_cm, i), wide(x_cm, i), bc_r[:, :])
                for i in range(KT):
                    last = eng.tensor_sub(wide(u_cm, i), wide(u_cm, i),
                                          bc_mr[:, :])
                return last
            P.step("vector", v_u, reads=["x", "bc"], writes=["u"])
            if ci == 0:
                dump("u", lambda: u_cm[:, :], ["u"])

            def win_evac(eng, oc, reg):
                if oc < NT_DIN:
                    return eng.activation(wide(z_cm, oc), reg, AF.Silu)
                elif oc < NT_DIN + NT_CONV:
                    i = oc - NT_DIN
                    return eng.activation(
                        xbc_cm[:, i * (C + 4) + 3:i * (C + 4) + 3 + C],
                        reg, AF.Copy)
                else:
                    return eng.activation(dtf[:, :], reg[0:NH, :], AF.Copy)

            win_bank = [0]
            def win_piece(p):
                slot, waits = cw(p)
                octs = ([32, 33] if p == 0 else
                        list(range((p - 1) * 4, (p - 1) * 4 + 4)))
                for half in range(len(octs) // 2):
                    ohs = octs[half * 2:half * 2 + 2]
                    ppn = ["pC", "pD"][win_bank[0] % 2]
                    pp = [pC, pD][win_bank[0] % 2]
                    win_bank[0] += 1
                    def t_win(eng, half=half, slot=slot, pp=pp):
                        last = None
                        for jj in range(2):
                            j = half * 2 + jj
                            for k in range(KT):
                                last = eng.matmul(
                                    pp[:, jj * 512:jj * 512 + 512],
                                    wring[slot][:, (j * KT + k) * 128:
                                                (j * KT + k + 1) * 128],
                                    wide(u_cm, k),
                                    start=(k == 0), stop=(k == KT - 1))
                        return last
                    P.step("tensor", t_win, reads=["u", f"wr{slot}"],
                           writes=[ppn], waits=waits if half == 0 else ())
                    wrs = sorted({"z" if oc < NT_DIN else
                                  ("xbc" if oc < NT_DIN + NT_CONV else "dtf")
                                  for oc in ohs})
                    if p >= 5:
                        wrs += ["e16a", "gw16a", "gw16b", "dsc"]
                    def s_ev(eng, ohs=ohs, pp=pp):
                        last = None
                        for jj, oc in enumerate(ohs):
                            last = win_evac(eng, oc,
                                            pp[:, jj * 512:jj * 512 + 512])
                        return last
                    P.step("scalar", s_ev, reads=[ppn, "xbc_tail"],
                           writes=wrs)
                issue()

            win_piece(0)
            def v_dt1(eng):
                eng.tensor_scalar_add(dtf[:, :], dtf[:, :], dtb_col[:, :])
                eng.tensor_scalar_mul(small[:, 0:C], dtf[:, :], -1.0)
                return eng.tensor_max(small[:, 0:C], small[:, 0:C], dtf[:, :])
            P.step("vector", v_dt1, reads=["dtf", "dtb"],
                   writes=["dtf", "small"])
            win_piece(1)
            def s_dt(eng):
                eng.activation(small[:, 0:C], small[:, 0:C], AF.Exp,
                               scale=cst[0:NH, 2:3])
                return eng.activation(small[:, 0:C], small[:, 0:C], AF.Ln,
                                      bias=cst[0:NH, 1:2])
            P.step("scalar", s_dt, reads=["small", "cst"], writes=["small"])
            win_piece(2)
            def v_dt2(eng):
                eng.tensor_scalar_max(dta[:, :], dtf[:, :], 0.0)
                eng.tensor_add(dtf[:, :], dta[:, :], small[:, 0:C])
                eng.tensor_scalar_mul(dta[:, :], dtf[:, :], acol[:, :])
                for qn in range(NSUB):
                    eng.tensor_tensor_scan(ellf[:, qn * Q:(qn + 1) * Q],
                                           dta[:, qn * Q:(qn + 1) * Q],
                                           zero32[:, :],
                                           0.0, ALU.add, ALU.add)
                eng.drain()
                for qn in range(NSUB):
                    eng.tensor_scalar(small[:, qn * Q:(qn + 1) * Q],
                                      ellf[:, qn * Q:(qn + 1) * Q],
                                      ellf[:, qn * Q + Q - 1:qn * Q + Q],
                                      -1.0, ALU.subtract, ALU.mult)
                return eng.tensor_copy(dt16_cm[:, :], dtf[:, :])
            P.step("vector", v_dt2, reads=["dtf", "small", "acol", "z32"],
                   writes=["dtf", "dta", "ellf", "small", "dt16"])
            win_piece(3)
            def s_dt2(eng):
                qdma(eng, "q_s", ell_dram.ap(), ellf[:, :])
                eng.activation(lam_cm[:, :], ellf[:, :], AF.Exp)
                return eng.activation(esc_cm[:, :], small[:, 0:C], AF.Exp)
            P.step("scalar", s_dt2, reads=["ellf", "small"],
                   writes=["lam", "esc", "ell_dram"])
            elwv = qdecl("q_s", 1)
            if ci == 0:
                dump("dt", lambda: dtf[:, :], ["dtf"])
                dump("ell", lambda: ellf[:, :], ["ellf"])
            win_piece(4)
            emit_ssd_chunk(ci)
            for p in range(5, 9):
                win_piece(p)
            if ci == 0:
                dump("zx", lambda: z_cm[:, :], ["z"])
                dump("xbc", lambda: xbc_cm[:, :].rearrange(
                    "p (i c) -> p i c", c=C + 4)[:, :, 3:3 + C], ["xbc"])

            for i in range(NT_CONV):
                ppn = ["pC", "pD"][i % 2]
                pp = [pC, pD][i % 2]
                def t_cv(eng, i=i, pp=pp):
                    last = None
                    for k in range(4):
                        last = eng.matmul(
                            pp[:, 0:512],
                            cdiag[:, (i * 4 + k) * 128:(i * 4 + k + 1) * 128],
                            xbc_cm[:, i * (C + 4) + k:i * (C + 4) + k + C],
                            start=(k == 0), stop=(k == 3))
                    return last
                P.step("tensor", t_cv, reads=["xbc", "xbc_tail", "cdiag"],
                       writes=[ppn])
                def s_cv(eng, i=i, pp=pp):
                    dst = wide(xs_cm, i) if i < NT_DIN else bcc_cm[:, :]
                    return eng.activation(dst, pp[:, 0:512], AF.Silu,
                                          bias=convb_c[:, i:i + 1])
                P.step("scalar", s_cv, reads=[ppn, "convb"],
                       writes=["xs" if i < NT_DIN else "bcc"])
            def g_ccc(eng):
                qdma(eng, "q_g", ccc_cm[:, :], bcc_cm[64:128, :])
            P.step("gpsimd", g_ccc, reads=["bcc"], writes=["ccc"])
            cccwv = qdecl("q_g", 1)
            def v_cccw(eng, cccwv=cccwv):
                eng.wait_ge(QS["q_g"], cccwv)
                return None
            P.step("vector", v_cccw, reads=["ccc"], writes=["ccc"])

            def v_tsave(eng):
                last = None
                for i in range(NT_CONV):
                    last = eng.tensor_copy(
                        tailbuf[:, i * 3:(i + 1) * 3],
                        xbc_cm[:, i * (C + 4) + C:i * (C + 4) + C + 3])
                return last
            P.step("vector", v_tsave, reads=["xbc"], writes=["tailbuf"])
            emit_ssd_g(ci)
            emit_ssd_xs(ci, 0)
            if ci == 0:
                dump("xs", lambda: xs_cm[:, :], ["xs"])
                dump("bcc", lambda: bcc_cm[:, :], ["bcc"])

            emit_ssd_early(ci, 0, elwv, xs_done=True)
            emit_ssd_late(ci, 0)
            for qn in range(NSUB):
                if qn + 1 < NSUB:
                    emit_ssd_early(ci, qn + 1, elwv)
                emit_ssd_y(ci, qn)
                if ci == 0 and qn == 0:
                    dump("hst0", lambda: h16[1][:, :], ["h16b"])
                if qn + 1 < NSUB:
                    emit_ssd_late(ci, qn + 1)

            if ci == 0:
                dump("g", lambda: g_cm[:, :], ["g"])
                dump("hst", lambda: h16[NSUB % 2][:, :],
                     ["h16a" if NSUB % 2 == 0 else "h16b"])

            bco = stats(g_cm, "g", NT_DIN, False, DIN, z_cm, "z",
                        defer_bc=True)
            for p in range(4):
                slot, waits = cw(9 + p)
                for j in range(2):
                    oc = p * 2 + j
                    ppn = ["pC", "pD"][oc % 2]
                    pp = [pC, pD][oc % 2]
                    def t_wo(eng, j=j, slot=slot, pp=pp):
                        last = None
                        for k in range(NT_DIN):
                            last = eng.matmul(
                                pp[:, 0:512],
                                wring[slot][:, (j * NT_DIN + k) * 128:
                                            (j * NT_DIN + k + 1) * 128],
                                wide(g_cm, k),
                                start=(k == 0), stop=(k == NT_DIN - 1))
                        return last
                    P.step("tensor", t_wo, reads=["g", f"wr{slot}"],
                           writes=[ppn], waits=waits if j == 0 else ())
                    if p == 0 and j == 0:
                        bco()
                    def v_wo(eng, oc=oc, pp=pp):
                        return eng.tensor_mul(wide(h1_cm, oc), pp[:, 0:512],
                                              bc_r[:, :])
                    P.step("vector", v_wo, reads=[ppn, "bc"], writes=["h1"])
                issue()
            def v_h1(eng):
                last = None
                for i in range(KT):
                    last = eng.tensor_add(wide(h1_cm, i), wide(h1_cm, i),
                                          wide(u_cm, i))
                return last
            P.step("vector", v_h1, reads=["h1", "u"], writes=["h1"])
            if ci == 0:
                dump("h1", lambda: h1_cm[:, :], ["h1"])

            stats(h1_cm, "h1", KT, False, D, moe_cm, "moe")
            def v_hn(eng):
                last = None
                for i in range(KT):
                    last = eng.tensor_mul(wide(hn_cm, i), wide(h1_cm, i),
                                          bc_r[:, :])
                return last
            P.step("vector", v_hn, reads=["h1", "bc"], writes=["hn"])

            for p in range(8):
                slot, waits = cw(13 + p)
                for j in range(4):
                    oc = p * 4 + j
                    ppn = ["pA", "pB"][oc % 2]
                    pp = [pA, pB][oc % 2]
                    def t_m1(eng, j=j, slot=slot, pp=pp):
                        last = None
                        for k in range(KT):
                            last = eng.matmul(
                                pp[:, 0:512],
                                wring[slot][:, (j * KT + k) * 128:(j * KT + k + 1) * 128],
                                wide(hn_cm, k),
                                start=(k == 0), stop=(k == KT - 1))
                        return last
                    P.step("tensor", t_m1, reads=["hn", f"wr{slot}"],
                           writes=[ppn], waits=waits if j == 0 else ())
                    def s_m1(eng, oc=oc, pp=pp):
                        dst = wide(z_cm, oc) if oc < 16 else wide(g_cm, oc - 16)
                        return eng.activation(dst, pp[:, 0:512], AF.Silu,
                                              bias=b1cols[:, oc:oc + 1])
                    P.step("scalar", s_m1, reads=[ppn, "b1c"],
                           writes=["z" if oc < 16 else "g"])
                issue()
            for grp in range(4):
                for o in range(2):
                    slot, waits = cw(21 + grp * 2 + o)
                    ppn = ["pC", "pD"][o]
                    pp = [pC, pD][o]
                    def t_m2(eng, slot=slot, pp=pp):
                        last = None
                        for k in range(32):
                            rhs = (wide(z_cm, k) if k < 16
                                   else wide(g_cm, k - 16))
                            last = eng.matmul(
                                pp[:, 0:512],
                                wring[slot][:, k * 128:(k + 1) * 128], rhs,
                                start=(k == 0), stop=(k == 31))
                        return last
                    P.step("tensor", t_m2, reads=["z", "g", f"wr{slot}"],
                           writes=[ppn], waits=waits)
                    def v_m2(eng, oc=grp * 2 + o, pp=pp):
                        return eng.scalar_tensor_tensor(
                            wide(u_cm, oc), pp[:, 0:512], b2cols[:, oc:oc + 1],
                            wide(h1_cm, oc), ALU.add, ALU.add)
                    P.step("vector", v_m2, reads=[ppn, "h1", "b2c"],
                           writes=["u"])
                issue(); issue()
            if ci == 0:
                dump("at", lambda: u_cm[:, :], ["u"])

            stats(u_cm, "u", KT, True, D, moe_cm, "moe")
            def v_x1(eng):
                last = None
                for i in range(KT):
                    eng.tensor_mul(wide(x1_cm, i), wide(u_cm, i), bc_r[:, :])
                for i in range(KT):
                    eng.tensor_sub(wide(x1_cm, i), wide(x1_cm, i), bc_mr[:, :])
                for i in range(KT):
                    last = eng.tensor_add(wide(x1_cm, i), wide(x1_cm, i),
                                          wide(x_cm, i))
                return last
            P.step("vector", v_x1, reads=["u", "x", "bc"],
                   writes=["x1", "xsT", "dtx"])
            if ci == 0:
                dump("x1", lambda: x1_cm[:, :], ["x1"])

            stats(x1_cm, "x1", KT, True, D, moe_cm, "moe")
            def v_x1n(eng):
                last = None
                for i in range(KT):
                    eng.tensor_mul(wide(hn_cm, i), wide(x1_cm, i), bc_r[:, :])
                for i in range(KT):
                    last = eng.tensor_sub(wide(hn_cm, i), wide(hn_cm, i),
                                          bc_mr[:, :])
                return last
            P.step("vector", v_x1n, reads=["x1", "bc"], writes=["hn"])
            if ci == 0:
                dump("x1n", lambda: hn_cm[:, :], ["hn"])

            def t_lg(eng):
                last = None
                for k in range(KT):
                    last = eng.matmul(pA[0:8, :], gatew_sb[:, k * 8:(k + 1) * 8],
                                      wide(hn_cm, k),
                                      start=(k == 0), stop=(k == KT - 1))
                return last
            P.step("tensor", t_lg, reads=["hn", "gw_sb"], writes=["pA"])
            def v_lg(eng):
                return eng.tensor_copy(small[0:8, 0:C], pA[0:8, :])
            P.step("vector", v_lg, reads=["pA"], writes=["small"],
                   dpre=True)
            def t_lgT(eng):
                last = None
                for tt in range(4):
                    last = eng.transpose(pB[:, tt * 8:(tt + 1) * 8],
                                         small[0:8, tt * 128:(tt + 1) * 128],
                                         i128f[0:8, 0:8])
                return last
            P.step("tensor", t_lgT, reads=["small", "i128f"], writes=["pB"])
            def v_lt(eng):
                return eng.tensor_copy(logT[:, :], pB[:, 0:32])
            P.step("vector", v_lt, reads=["pB"], writes=["logT"],
                   dpre=True)

            def T3(t):
                return t[:, :].rearrange("p (i e) -> p i e", e=NE)
            def v_r1(eng):
                return eng.reduce_max(rt_m[:, 0:4], T3(logT), axis=AX.X)
            P.step("vector", v_r1, reads=["logT"], writes=["rt_m"],
                   dpre=True)
            def v_r2(eng):
                eng.tensor_tensor(T3(rt_e), T3(logT),
                                  rt_m[:, 0:4, None].broadcast_to([128, 4, NE]),
                                  ALU.is_ge)
                eng.scalar_tensor_tensor(rt_s[:, :], rt_e[:, :], -1e30,
                                         logT[:, :], ALU.mult, ALU.add)
                return eng.reduce_max(rt_m[:, 4:8], T3(rt_s), axis=AX.X)
            P.step("vector", v_r2, reads=["logT", "rt_m"],
                   writes=["rt_m", "rt_e", "rt_s"], dpre=True)
            def v_r3(eng):
                return eng.tensor_tensor(
                    T3(rt_e), T3(logT),
                    rt_m[:, 0:4, None].broadcast_to([128, 4, NE]), ALU.subtract)
            P.step("vector", v_r3, reads=["logT", "rt_m"], writes=["rt_e"],
                   dpre=True)
            def s_r(eng):
                return eng.activation(rt_e[:, :], rt_e[:, :], AF.Exp)
            P.step("scalar", s_r, reads=["rt_e"], writes=["rt_e"])
            def v_r4(eng):
                return eng.reduce_sum(rt_m[:, 0:4], T3(rt_e), axis=AX.X)
            P.step("vector", v_r4, reads=["rt_e"], writes=["rt_m"],
                   dpre=True)
            def v_r5(eng):
                return eng.reciprocal(rt_m[:, 0:4], rt_m[:, 0:4])
            P.step("vector", v_r5, reads=["rt_m"], writes=["rt_m"],
                   dpre=True)
            def v_r6(eng):
                eng.tensor_tensor(T3(rt_s), T3(logT),
                                  rt_m[:, 4:8, None].broadcast_to([128, 4, NE]),
                                  ALU.is_ge)
                eng.tensor_mul(rt_s[:, :], rt_s[:, :], rt_e[:, :])
                return eng.tensor_tensor(
                    T3(rt_s), T3(rt_s),
                    rt_m[:, 0:4, None].broadcast_to([128, 4, NE]), ALU.mult)
            P.step("vector", v_r6, reads=["logT", "rt_m", "rt_e"],
                   writes=["rt_s"], dpre=True)
            if ci == 0:
                dump("sc", lambda: rt_s[:, :], ["rt_s"])
            def t_sT(eng):
                return eng.transpose(pB[0:32, 0:128], rt_s[:, :], i128f[:, :])
            P.step("tensor", t_sT, reads=["rt_s", "i128f"], writes=["pB"])
            def v_sT(eng):
                return eng.tensor_copy(sc_cmT[:, :], pB[0:32, 0:128])
            P.step("vector", v_sT, reads=["pB"], writes=["sc_cmT"],
                   dpre=True)
            def s_sd(eng):
                qdma(eng, "q_s", sc_dram.ap(), sc_cmT[:, :])
            P.step("scalar", s_sd, reads=["sc_cmT"], writes=["sc_dram"])
            scwv = qdecl("q_s", 1)

            def ge_tile(t):
                return wide(z_cm, t) if t < 16 else wide(g_cm, t - 16)
            for e in range(NE):
                base = 29 + e * 2
                s1, w1 = cw(base + 0)
                def g_sr(eng, e=e, scwv=scwv):
                    eng.wait_ge(QS["q_s"], scwv)
                    qdma(eng, "q_g",
                         srow[0:1, :].rearrange("p (i t) -> p i t", t=128),
                         sc_dram.ap().rearrange("(i e) t -> e i t",
                                                e=NE)[e][None, :, :])
                P.step("gpsimd", g_sr, reads=["sc_dram"], writes=["srow"])
                srwv = qdecl("q_g", 1)
                def t_sb(eng, srwv=srwv):
                    eng.wait_ge(QS["q_g"], srwv)
                    return eng.matmul(pA[:, :], ones_b[:, :], srow[0:1, :],
                                      start=True, stop=True)
                P.step("tensor", t_sb, reads=["srow", "ones"], writes=["pA"])
                def v_sb(eng):
                    return eng.tensor_copy(sbc[:, :], pA[:, :])
                P.step("vector", v_sb, reads=["pA"], writes=["h1"])

                def preg4(j):
                    pp = pC if j < 2 else pD
                    return pp[:, (j % 2) * 512:(j % 2) * 512 + 512]
                s3, w33 = cw(base + 1)
                for hf, (pp, ppn) in enumerate(((pC, "pC"), (pD, "pD"))):
                    def t_p1(eng, hf=hf, s1=s1, pp=pp):
                        last = None
                        for jj in range(2):
                            j = hf * 2 + jj
                            for k in range(KT):
                                last = eng.matmul(
                                    pp[:, jj * 512:jj * 512 + 512],
                                    wring[s1][:, (j * KT + k) * 128:
                                              (j * KT + k + 1) * 128],
                                    wide(hn_cm, k),
                                    start=(k == 0), stop=(k == KT - 1))
                        return last
                    P.step("tensor", t_p1, reads=["hn", f"wr{s1}"],
                           writes=[ppn], waits=w1 if hf == 0 else ())
                    def v_p1(eng, hf=hf, pp=pp):
                        eng.tensor_tensor(p1s[:, hf * 1024:hf * 1024 + 512],
                                          pp[:, 0:512], sbc[:, :], ALU.mult)
                        return eng.tensor_tensor(
                            p1s[:, hf * 1024 + 512:hf * 1024 + 1024],
                            pp[:, 512:1024], sbc[:, :], ALU.mult)
                    P.step("vector", v_p1, reads=[ppn, "h1"], writes=["u"])
                    def s_p1(eng, hf=hf):
                        return eng.activation(
                            p1s[:, hf * 1024:(hf + 1) * 1024],
                            p1s[:, hf * 1024:(hf + 1) * 1024], AF.Silu)
                    P.step("scalar", s_p1, reads=["u"], writes=["u"])
                issue()
                for hf, (pp, ppn) in enumerate(((pC, "pC"), (pD, "pD"))):
                    def t_p3(eng, hf=hf, s3=s3, pp=pp):
                        last = None
                        for jj in range(2):
                            j = hf * 2 + jj
                            for k in range(KT):
                                last = eng.matmul(
                                    pp[:, jj * 512:jj * 512 + 512],
                                    wring[s3][:, (j * KT + k) * 128:
                                              (j * KT + k + 1) * 128],
                                    wide(hn_cm, k),
                                    start=(k == 0), stop=(k == KT - 1))
                        return last
                    P.step("tensor", t_p3, reads=["hn", f"wr{s3}"],
                           writes=[ppn], waits=w33 if hf == 0 else ())
                    def v_p3(eng, hf=hf, pp=pp):
                        eng.tensor_tensor(p3s[:, hf * 1024:hf * 1024 + 512],
                                          pp[:, 0:512], sbc[:, :], ALU.mult)
                        return eng.tensor_tensor(
                            p3s[:, hf * 1024 + 512:hf * 1024 + 1024],
                            pp[:, 512:1024], sbc[:, :], ALU.mult)
                    P.step("vector", v_p3, reads=[ppn, "h1"], writes=["u"])
                    def g_ge(eng, e=e, hf=hf):
                        eng.tensor_mul(ge_tile(e * 4 + hf * 2),
                                       p1s[:, hf * 1024:hf * 1024 + 512],
                                       p3s[:, hf * 1024:hf * 1024 + 512])
                        return eng.tensor_mul(
                            ge_tile(e * 4 + hf * 2 + 1),
                            p1s[:, hf * 1024 + 512:hf * 1024 + 1024],
                            p3s[:, hf * 1024 + 512:hf * 1024 + 1024])
                    P.step("vector", g_ge, reads=["u"], writes=["z", "g"],
                           dpre=True)
                issue()
            for og in range(2):
                for e in range(NE):
                    s2, w2 = cw(29 + 16 + og * NE + e)
                    def t_w2(eng, e=e, og=og, s2=s2, w2=w2):
                        for qn_, v in w2:
                            eng.wait_ge(QS[qn_], v)
                        last = None
                        for oj in range(4):
                            oct = og * 4 + oj
                            pp = pC if oj < 2 else pD
                            for k in range(4):
                                last = eng.matmul(
                                    pp[:, (oj % 2) * 512:(oj % 2) * 512 + 512],
                                    wring[s2][:, (oct * 4 + k) * 128:(oct * 4 + k + 1) * 128],
                                    ge_tile(e * 4 + k),
                                    start=(e == 0 and k == 0),
                                    stop=(e == NE - 1 and k == 3))
                        return last
                    P.step("tensor", t_w2, reads=["z", "g", f"wr{s2}"],
                           writes=["pC", "pD"])
                    issue()
                def v_ym(eng, og=og):
                    last = None
                    for oj in range(4):
                        pp = pC if oj < 2 else pD
                        last = eng.tensor_copy(
                            wide(moe_cm, og * 4 + oj),
                            pp[:, (oj % 2) * 512:(oj % 2) * 512 + 512])
                    return last
                P.step("vector", v_ym, reads=["pC", "pD"], writes=["moe"])
            if ci == 0:
                dump("moe", lambda: moe_cm[:, :], ["moe"])

            stats(moe_cm, "moe", KT, True, D, z_cm, "z")
            def v_x2(eng):
                last = None
                for i in range(KT):
                    eng.tensor_mul(wide(x_cm, i), wide(moe_cm, i), bc_r[:, :])
                for i in range(KT):
                    eng.tensor_sub(wide(x_cm, i), wide(x_cm, i), bc_mr[:, :])
                for i in range(KT):
                    last = eng.tensor_add(wide(x_cm, i), wide(x_cm, i),
                                          wide(x1_cm, i))
                return last
            P.step("vector", v_x2, reads=["moe", "x1", "bc"], writes=["x"])

            for tt in range(C // 128):
                def t_ot(eng, tt=tt):
                    last = None
                    for k in range(KT):
                        last = eng.transpose(
                            pT[:, k * 128:(k + 1) * 128],
                            wide(x_cm, k, (tt * 128, tt * 128 + 128)),
                            i128b[:, :])
                    return last
                P.step("tensor", t_ot, reads=["x", "i128b"], writes=["pT"])
                def v_ot(eng, tt=tt):
                    return eng.tensor_copy(x2t32[tt % 2][:, :], pT[:, :])
                wvp = store_wv[tt % 2]
                P.step("vector", v_ot, reads=["pT"], writes=["hn"],
                       waits=([("q_s", wvp)] if wvp else ()))
                def s_ot(eng, tt=tt, tok0=tok0):
                    qdma(eng, "q_s",
                         out_t.ap()[tok0 + tt * 128:tok0 + (tt + 1) * 128, :],
                         x2t32[tt % 2][:, :])
                P.step("scalar", s_ot, reads=["hn"], writes=["outq"])
                store_wv[tt % 2] = qdecl("q_s", 1)

        def fin(eng):
            eng.wait_ge(QS["q_s"], qcnt["q_s"])
            eng.wait_ge(QS["q_w"], qcnt["q_w"])
            eng.wait_ge(QS["q_g"], qcnt["q_g"])
            return None
        P.step("scalar", fin, reads=["outq", "dbgq"], writes=[])

        ordinal = {}
        count = {e: 0 for e in ENGINES}
        for sid, s in enumerate(P.steps):
            count[s["eng"]] += 1
            ordinal[sid] = count[s["eng"]]
        for sid, s in enumerate(P.steps):
            w = {}
            deps = set(s["deps"])
            if serial and sid > 0:
                deps.add(sid - 1)
            for d in deps:
                ds = P.steps[d]
                if ds["eng"] == s["eng"]:
                    continue
                w[ds["eng"]] = max(w.get(ds["eng"], 0), ordinal[d])
            s["semwaits"] = w

        with nc.Block() as blk:
            for ename in ENGINES:
                def emit_engine(eng, ename=ename):
                    for s in P.steps:
                        if s["eng"] != ename:
                            continue
                        if s["dpre"]:
                            eng.drain()
                        for se, v in s["semwaits"].items():
                            eng.wait_ge(esem[se], v)
                        for qn_, v in s["waits"]:
                            eng.wait_ge(QS[qn_], v)
                        last = s["fn"](eng)
                        if last is None:
                            last = eng.nop()
                        last.then_inc(esem[ename], 1)
                getattr(blk, ename)(emit_engine)
        for qn_ in qcnt:
            assert qrt[qn_] == qcnt[qn_], (qn_, qrt[qn_], qcnt[qn_])
    return nc, dbg


def make_consts():
    i128 = np.eye(128, dtype=np.float32)
    m01 = np.triu(np.ones((128, 128), dtype=np.float32))
    r01 = np.zeros((NH, 4096), dtype=np.float32)
    for f in range(4096):
        h = 16 * (f // 2048) + (f % 2048) // Q
        r01[h, f] = -1.0
    slot = np.zeros((NH, 128), dtype=np.float32)
    for h in range(NH):
        for p in range(128):
            if h % 2 == p // 64:
                slot[h, p] = 1.0
    r02 = np.zeros((NH, 16), dtype=np.float32)
    for h in range(NH):
        r02[h, h // 2] = 1.0
    return dict(c_i128=i128, c_m01=m01, c_r01=r01, c_slot=slot, c_r02=r02)


def make_inmap(inputs, b, L=8192):
    m = dict(
        x=np.ascontiguousarray(inputs["L0_attn_hidden_tokens"][b, :L]),
        W_in=inputs["L0_attn_W_in"], conv_w=inputs["L0_attn_conv_w"],
        conv_b=inputs["L0_attn_conv_b"], dt_bias=inputs["L0_attn_dt_bias"],
        A_log=inputs["L0_attn_A_log"], D_param=inputs["L0_attn_D_param"],
        norm_w=inputs["L0_attn_norm_w"], W_out=inputs["L0_attn_W_out"],
        rms_w=inputs["L0_attn_rms_w"], mlp_w1=inputs["L0_attn_mlp_w1"],
        mlp_b1=inputs["L0_attn_mlp_b1"], mlp_w2=inputs["L0_attn_mlp_w2"],
        mlp_b2=inputs["L0_attn_mlp_b2"], gate_w=inputs["L0_ffn_gate_w"],
        moe_w1=inputs["L0_ffn_w1"], moe_w2=inputs["L0_ffn_w2"],
        moe_w3=inputs["L0_ffn_w3"])
    m.update(make_consts())
    return {k: np.asarray(v, dtype=np.float32) for k, v in m.items()}


def kernel(**inputs):
    L = 8192
    nc, _ = build_nc(L=L)
    in_maps = [make_inmap(inputs, b, L) for b in range(8)]
    res = run_bass_kernel_spmd(nc, in_maps, core_ids=list(range(8)))
    out = np.stack([np.asarray(r["out"]) for r in res.results], axis=0)
    return out.astype(np.float32)
